# version 14
# baseline (speedup 1.0000x reference)
import numpy as np
import concourse.bass as bass
import concourse.mybir as mybir
from concourse.bass_utils import run_bass_kernel_spmd

F32 = mybir.dt.float32
AF = mybir.ActivationFunctionType
ALU = mybir.AluOpType
AX = mybir.AxisListType

D = 2048
HID = 8192
KD = D // 128
KH = HID // 128
NHEAD = 16
NGRP = 4
GSZ = NHEAD // NGRP
DH = 128
WIN = 512
SINK = 4
EPS = 1e-5
SCALE = float(DH) ** -0.5
NEG = -1.0e30
FFN_GROUPS = [3] * 21 + [1]
GRP_MAX = 3
BF16_GU = True
BF16_DOWN = True

ENGS = ("sync", "tensor", "scalar", "vector")
SWAP_ADJ = [i ^ 1 for i in range(32)]


class Sem:
    __slots__ = ("h", "n")

    def __init__(self, nc, name):
        self.h = nc.alloc_semaphore(name=name)
        self.n = 0


class W:

    def __init__(self, eng_name, eng):
        self.e = eng_name
        self.eng = eng

    def mm(self, out, lhsT, rhs, start, stop, inc=None):
        if self.e == "tensor":
            ins = self.eng.matmul(out, lhsT, rhs, start=start, stop=stop)
            if inc is not None:
                ins.then_inc(inc.h, 1)
        if inc is not None:
            inc.n += 1

    def tr(self, out, in_, ident, inc=None):
        if self.e == "tensor":
            ins = self.eng.transpose(out, in_, ident)
            if inc is not None:
                ins.then_inc(inc.h, 1)
        if inc is not None:
            inc.n += 1

    def op(self, engine, fn, inc=None, inck=1):
        if self.e == engine:
            ins = fn(self.eng)
            if inc is not None:
                ins.then_inc(inc.h, inck)
        if inc is not None:
            inc.n += inck

    def dma(self, out, in_, inc=None):
        if self.e == "sync":
            ins = self.eng.dma_start(out=out, in_=in_)
            if inc is not None:
                ins.then_inc(inc.h, 16)
        if inc is not None:
            inc.n += 16

    def wait(self, engine, sem, val=None):
        if self.e == engine:
            self.eng.wait_ge(sem.h, sem.n if val is None else val)

    def drain(self, engine):
        if self.e == engine:
            self.eng.drain()


def balanced_split(total, cap=512):
    n = -(-total // cap)
    base = -(-total // n)
    out, o = [], 0
    while o < total:
        c = min(base, total - o)
        out.append((o, c))
        o += c
    return out


class Builder:
    def __init__(self, own):
        self.OWN = own
        self.ST = own // 2
        self.EXT1 = WIN + self.ST
        self.R2 = self.ST + SINK
        self.NQT = self.ST // 128
        self.KV = WIN + self.ST
        self.TTOT = self.EXT1 + self.ST + SINK
        self.MCOL = WIN + 128 + SINK
        nc = bass.Bass("TRN2", target_bir_lowering=False)
        self.nc = nc

        def t(n, s):
            return nc.dram_tensor(n, s, F32, kind="ExternalInput").ap()

        self.xe = t("xe", [D, self.EXT1])
        self.xo2 = t("xo2", [D, self.ST])
        self.xsink = t("xsink", [D, SINK])
        self.wgt = t("wgt", [D, HID])
        self.wut = t("wut", [D, HID])
        self.wdt = t("wdt", [HID, D])
        self.wq = t("wq", [D, D])
        self.wk = t("wk", [D, NGRP * DH])
        self.wv = t("wv", [D, NGRP * DH])
        self.wo = t("wo", [D, D])
        self.bqc = t("bqc", [128, KD])
        self.bkc = t("bkc", [128, NGRP])
        self.bvc = t("bvc", [128, NGRP])
        self.boc = t("boc", [128, KD])
        self.cos2 = t("cos2", [128, self.TTOT])
        self.sin2 = t("sin2", [128, self.TTOT])
        self.maskd = t("maskd", [2 * self.NQT * 128, self.MCOL])
        self.yt = nc.dram_tensor("yt", [D, self.OWN], F32, kind="ExternalOutput").ap()

        self.ACOLS = 52600
        self.sb = nc.alloc_sbuf_tensor("arena", [128, self.ACOLS], F32)
        o = [0]

        def region(n):
            r = o[0]
            o[0] += n
            return r

        self.RING1 = region(KD * self.EXT1)
        self.RING2 = region(KD * self.R2)
        PH = region(24832)
        self.PHASE = PH
        self.IDENT = region(128)
        self.ONESC = region(1)
        self.ONESR = region(128)
        self.BIASC = region(2 * KD + 2 * NGRP)
        self.KSINK = region(NGRP * SINK)
        self.VSINK = region(NGRP * DH)
        self.SROW = region(2 * 528)
        self.SQT = region(1024)
        self.NMX = region(4)
        self.GT2 = self.SQT
        assert o[0] <= self.ACOLS, o[0]

        self.F_XN = PH
        self.F_GATED = PH + KD * 512
        self.F_WGU = self.F_GATED + GRP_MAX * 512
        self.F_WD = self.F_WGU + 2 * (2 * KD * 128)
        assert self.F_WD + GRP_MAX * D - PH <= 24832
        self.P_Q = PH
        self.P_K = self.P_Q + KD * self.ST
        self.P_V = self.P_K + NGRP * self.KV
        self.P_WS = self.P_V + (self.KV // 128) * 512
        self.P_SUM = self.P_WS + 2 * KD * 128
        SCR = self.P_SUM + 1152
        self.CS = SCR
        self.RSB = SCR + 1024
        self.RT = SCR + 1536
        self.P_MASK = SCR
        self.P_P = SCR + 648
        self.P_PN = SCR + 1296
        self.P_PT = SCR + 1944
        assert SCR + 3072 - PH <= 24832, SCR + 3072 - PH

        self.ps = nc.alloc_psum_tensor("psarena", [128, 4096], F32)
        self.bb = self.sb.bitcast(mybir.dt.bfloat16)
        self.sc_wgu = nc.dram_tensor("sc_wgu", [KH, 128, 4 * KD * 128],
                                     mybir.dt.bfloat16, kind="Internal")
        self.sc_wd = nc.dram_tensor("sc_wd", [KH, 128, 2 * D],
                                    mybir.dt.bfloat16, kind="Internal")
        self.GTMP = self.PHASE + 24064
        self.sems = {}
        self._consts = {}

    def apc(self, base, cols, part0=0, parts=128):
        return self.sb.ap()[part0:part0 + parts, base:base + cols]

    def ap3(self, base, d1, n1, d2, n2):
        return bass.AP(self.sb, base, [[self.ACOLS, 128], [d1, n1], [d2, n2]])

    def bapc(self, bcol, cols, part0=0, parts=128):
        return self.bb.ap()[part0:part0 + parts, bcol:bcol + cols]

    def bp3(self, bcol, d1, n1, d2, n2):
        return bass.AP(self.bb, bcol, [[2 * self.ACOLS, 128], [d1, n1], [d2, n2]])

    def pb(self, bank, cols=512, c0=0, parts=128):
        return self.ps.ap()[0:parts, bank * 512 + c0:bank * 512 + c0 + cols]

    def psp(self, c0, cols):
        return self.ps.ap()[:, c0:c0 + cols]

    def ring1(self, k, c0, c):
        return self.apc(self.RING1 + k * self.EXT1 + c0, c)

    def ring2(self, k, c0, c):
        return self.apc(self.RING2 + k * self.R2 + c0, c)

    def sem(self, name):
        if name not in self.sems:
            self.sems[name] = Sem(self.nc, name)
        return self.sems[name]

    def snap(self, tag, names):
        return {s: self.sem(tag + s).n for s in names}

    def const(self, name, arr):
        if name not in self._consts:
            self._consts[name] = self.nc.inline_tensor(np.ascontiguousarray(arr), name)
        return self._consts[name]

    def phase(self, name, emit):
        presnap = {k: s.n for k, s in self.sems.items()}
        counts = None
        with self.nc.Block(name) as blk:
            for e in ENGS:
                for k, s in self.sems.items():
                    s.n = presnap.get(k, 0)
                w = W(e, getattr(self.nc, e))
                getattr(blk, e)(lambda eng, w=w: emit(w))
                chk = {k: s.n for k, s in self.sems.items()}
                if counts is None:
                    counts = chk
                else:
                    assert chk == counts, f"walk divergence in {name}"

    def emit_preload(self, w):
        sl = self.sem("pre")
        w.dma(self.apc(self.IDENT, 128), self.const("identc", np.eye(128, dtype=np.float32)).ap(), inc=sl)
        w.dma(self.apc(self.ONESC, 1), self.const("onescc", np.ones((128, 1), np.float32)).ap(), inc=sl)
        w.dma(self.apc(self.ONESR, 128, parts=1), self.const("onesrc", np.ones((1, 128), np.float32)).ap(), inc=sl)
        w.dma(self.apc(self.BIASC, KD), self.bqc, inc=sl)
        w.dma(self.apc(self.BIASC + KD, KD), self.boc, inc=sl)
        w.dma(self.apc(self.BIASC + 2 * KD, NGRP), self.bkc, inc=sl)
        w.dma(self.apc(self.BIASC + 2 * KD + NGRP, NGRP), self.bvc, inc=sl)
        for e in ("tensor", "scalar", "vector"):
            w.wait(e, sl)

    def ln_stats(self, w, tag, src, C, bcast_m=True):
        sq = self.sem(tag + "sq")
        st = self.sem(tag + "st")
        vr = self.sem(tag + "vr")
        rs = self.sem(tag + "rs")
        bc = self.sem(tag + "bc")
        b = self.snap(tag, ("sq", "st", "vr", "rs", "bc"))
        mrow = self.apc(self.SROW, C, parts=1)
        negm = mrow if not bcast_m else None
        rrow = self.apc(self.SROW + 528, C, parts=1)
        t2 = self.apc(self.SQT, C, parts=1)
        w.wait("vector", bc, b["bc"])
        w.wait("tensor", vr, b["vr"])
        for k in range(KD):
            tsl = self.apc(self.SQT + (k % 2) * 512, C)
            if k >= 2:
                w.wait("scalar", st, b["st"] + k - 1)
            w.op("scalar", lambda e, tsl=tsl, k=k: e.activation(out=tsl, in_=src(k), func=AF.Square), inc=sq)
            w.wait("tensor", sq, b["sq"] + k + 1)
            w.mm(self.pb(6, C, parts=1), self.apc(self.ONESC, 1), src(k), k == 0, k == KD - 1)
            w.mm(self.pb(7, C, parts=1), self.apc(self.ONESC, 1), tsl, k == 0, k == KD - 1,
                 inc=(st if True else None))
        w.wait("vector", st, b["st"] + KD)
        w.op("vector", lambda e: e.tensor_scalar_mul(mrow, self.pb(6, C, parts=1), 1.0 / D))
        w.op("vector", lambda e: e.tensor_scalar_mul(rrow, self.pb(7, C, parts=1), 1.0 / D))
        w.drain("vector")
        w.op("vector", lambda e: e.tensor_tensor(t2, mrow, mrow, ALU.mult))
        w.drain("vector")
        if not bcast_m:
            w.op("vector", lambda e: e.tensor_scalar_mul(mrow, mrow, -1.0))
        w.op("vector", lambda e: e.tensor_sub(rrow, rrow, t2))
        w.drain("vector")
        w.op("vector", lambda e: e.tensor_scalar_add(rrow, rrow, EPS), inc=vr)
        w.wait("scalar", vr, b["vr"] + 1)
        w.op("scalar", lambda e: e.activation(out=t2, in_=rrow, func=AF.Ln))
        w.drain("scalar")
        w.op("scalar", lambda e: e.activation(out=rrow, in_=t2, func=AF.Exp, scale=-0.5), inc=rs)
        w.wait("tensor", rs, b["rs"] + 1)
        if bcast_m:
            w.mm(self.pb(4, C), self.apc(self.ONESR, 128, parts=1), mrow, True, True)
        w.mm(self.pb(5, C), self.apc(self.ONESR, 128, parts=1), rrow, True, True, inc=bc)
        return bc, mrow, negm, rrow

    def emit_ffn_tile(self, w, src, C, half=0.5, mode="f32"):
        tag = "A"
        names = ("xn", "w0", "w1", "wd", "mg", "mu", "si", "gt", "dn", "ac",
                 "ch", "cl", "so", "xc", "h2", "c2", "s2", "gh", "gl")
        sms = {s: self.sem(tag + s) for s in names}
        b = self.snap(tag, names)
        xnd, wdl = sms["xn"], sms["wd"]
        mmg, mmu, sil, gtd, dwn, acc = (sms[s] for s in ("mg", "mu", "si", "gt", "dn", "ac"))
        ch, cl, so, xc = (sms[s] for s in ("ch", "cl", "so", "xc"))
        h2, c2, s2, ghs, gls = (sms[s] for s in ("h2", "c2", "s2", "gh", "gl"))
        WGUB = 2 * self.F_WGU
        XH = 2 * self.F_XN
        XL = XH + KD * 512
        SPH = 2 * self.RING2
        SPL = SPH + 2 * KD * 128

        NG = len(FFN_GROUPS)
        gbase = [sum(FFN_GROUPS[:i]) for i in range(NG)]
        for g in range(NG):
            grp = FFN_GROUPS[g]
            for j in range(grp):
                hc = gbase[g] + j
                par = hc % 2
                w.wait("sync", mmu, max(0, b["mu"] + hc - 1))
                if mode == "bf16":
                    w.wait("sync", so, 32 * (hc + 1))
                    w.dma(self.bapc(WGUB + par * 8192, 8192),
                          bass.AP(self.sc_wgu, hc * 128 * 8192, [[8192, 128], [1, 8192]]),
                          inc=sms["w%d" % par])
                else:
                    if mode == "split":
                        w.wait("sync", cl, max(0, b["cl"] + hc - 1))
                    w.dma(self.ap3(self.F_WGU + par * 2 * KD * 128, 128, KD, 1, 128),
                          bass.AP(self.wgt.tensor, hc * 128, [[HID, 128], [128 * HID, KD], [1, 128]]), inc=sms["w%d" % par])
                    w.dma(self.ap3(self.F_WGU + par * 2 * KD * 128 + KD * 128, 128, KD, 1, 128),
                          bass.AP(self.wut.tensor, hc * 128, [[HID, 128], [128 * HID, KD], [1, 128]]), inc=sms["w%d" % par])
                if mode == "split":
                    if hc > 0:
                        w.wait("sync", cl, b["cl"] + hc)
                        w.dma(bass.AP(self.sc_wgu, (hc - 1) * 128 * 8192, [[8192, 128], [1, 4096]]),
                              self.bapc(SPH, 4096), inc=so)
                        w.dma(bass.AP(self.sc_wgu, (hc - 1) * 128 * 8192 + 4096, [[8192, 128], [1, 4096]]),
                              self.bapc(SPL, 4096), inc=so)
                    if hc == KH - 1:
                        w.wait("sync", cl, b["cl"] + hc + 1)
                        w.dma(bass.AP(self.sc_wgu, hc * 128 * 8192, [[8192, 128], [1, 4096]]),
                              self.bapc(SPH, 4096), inc=so)
                        w.dma(bass.AP(self.sc_wgu, hc * 128 * 8192 + 4096, [[8192, 128], [1, 4096]]),
                              self.bapc(SPL, 4096), inc=so)
            if mode == "bf16" and BF16_DOWN:
                w.wait("sync", dwn, max(0, b["dn"] + g * KD))
                w.wait("sync", s2, 16 * (gbase[g] + grp))
                w.dma(self.bp3(2 * self.F_WD, 2 * D, grp, 1, 2 * D),
                      bass.AP(self.sc_wd, gbase[g] * 128 * 2 * D, [[2 * D, 128], [128 * 2 * D, grp], [1, 2 * D]]),
                      inc=wdl)
            else:
                w.wait("sync", dwn, max(0, b["dn"] + g * KD))
                if mode == "split" and BF16_DOWN:
                    w.wait("sync", c2, max(0, b["c2"] + gbase[g]))
                w.dma(self.ap3(self.F_WD, D, grp, 1, D),
                      bass.AP(self.wdt.tensor, gbase[g] * 128 * D, [[D, 128], [128 * D, grp], [1, D]]),
                      inc=wdl)
                if mode == "split" and BF16_DOWN:
                    for j in range(grp):
                        hc = gbase[g] + j
                        w.wait("sync", c2, b["c2"] + hc + 1)
                        w.dma(bass.AP(self.sc_wd, hc * 128 * 2 * D, [[2 * D, 128], [1, 2 * D]]),
                              self.bapc(2 * self.RING2 + 4 * KD * 128, 2 * D), inc=s2)

        w.wait("scalar", gtd, b["gt"])
        w.wait("vector", mmu, b["mu"])

        bc, mrow, negm, rrow = self.ln_stats(w, tag + "l", src, C, bcast_m=True)
        w.wait("vector", bc)
        if mode == "bf16":
            for k in range(KD):
                tmp = self.apc(self.GT2 + (k % 2) * 512, C)
                w.op("vector", lambda e, tmp=tmp, k=k: e.tensor_sub(tmp, src(k), self.pb(4, C)))
                w.drain("vector")
                w.op("vector", lambda e, tmp=tmp: e.tensor_tensor(tmp, tmp, self.pb(5, C), ALU.mult))
                w.drain("vector")
                w.op("vector", lambda e, tmp=tmp, k=k: e.tensor_copy(self.bapc(XH + k * 512, C), tmp))
                w.drain("vector")
                w.op("vector", lambda e, tmp=tmp, k=k: e.tensor_sub(self.bapc(XL + k * 512, C), tmp, self.bapc(XH + k * 512, C)), inc=xnd)
            w.drain("vector")
        else:
            for k in range(KD):
                w.op("vector", lambda e, k=k: e.tensor_sub(self.apc(self.F_XN + k * 512, C), src(k), self.pb(4, C)))
            w.drain("vector")
            for k in range(KD):
                w.op("vector", lambda e, k=k: e.tensor_tensor(self.apc(self.F_XN + k * 512, C),
                                                              self.apc(self.F_XN + k * 512, C), self.pb(5, C), ALU.mult),
                     inc=xnd)
            w.drain("vector")

        wcount = 16 if mode == "bf16" else 32
        for g in range(NG):
            grp = FFN_GROUPS[g]
            for j in range(grp):
                hc = gbase[g] + j
                par = hc % 2
                gb, ub = par, 2 + par
                w.wait("tensor", sms["w%d" % par], b["w%d" % par] + wcount * (hc // 2 + 1))
                if hc >= 2:
                    w.wait("tensor", sil, b["si"] + hc - 1)
                    w.wait("tensor", gtd, b["gt"] + hc - 1)
                if mode == "bf16":
                    if hc == 0:
                        w.wait("tensor", xnd, b["xn"] + KD)
                    wb = WGUB + par * 8192
                    for k in range(KD):
                        xh_k = self.bapc(XH + k * 512, C)
                        xl_k = self.bapc(XL + k * 512, C)
                        w.mm(self.pb(gb, C), self.bapc(wb + k * 128, 128), xh_k, k == 0, False)
                        w.mm(self.pb(gb, C), self.bapc(wb + k * 128, 128), xl_k, False, False)
                        w.mm(self.pb(gb, C), self.bapc(wb + 4096 + k * 128, 128), xh_k, False, k == KD - 1,
                             inc=(mmg if k == KD - 1 else None))
                    for k in range(KD):
                        xh_k = self.bapc(XH + k * 512, C)
                        xl_k = self.bapc(XL + k * 512, C)
                        w.mm(self.pb(ub, C), self.bapc(wb + 2048 + k * 128, 128), xh_k, k == 0, False)
                        w.mm(self.pb(ub, C), self.bapc(wb + 2048 + k * 128, 128), xl_k, False, False)
                        w.mm(self.pb(ub, C), self.bapc(wb + 6144 + k * 128, 128), xh_k, False, k == KD - 1,
                             inc=(mmu if k == KD - 1 else None))
                else:
                    for k in range(KD):
                        if hc == 0:
                            w.wait("tensor", xnd, b["xn"] + k + 1)
                        w.mm(self.pb(gb, C), self.apc(self.F_WGU + par * 2 * KD * 128 + k * 128, 128),
                             self.apc(self.F_XN + k * 512, C), k == 0, k == KD - 1,
                             inc=(mmg if k == KD - 1 else None))
                    for k in range(KD):
                        w.mm(self.pb(ub, C), self.apc(self.F_WGU + par * 2 * KD * 128 + KD * 128 + k * 128, 128),
                             self.apc(self.F_XN + k * 512, C), k == 0, k == KD - 1,
                             inc=(mmu if k == KD - 1 else None))
                if mode == "split":
                    w.wait("scalar", sms["w%d" % par], b["w%d" % par] + wcount * (hc // 2 + 1))
                    w.wait("scalar", so, max(0, b["so"] + 32 * hc))
                    w.op("scalar", lambda e, par=par: e.activation(
                        out=self.bapc(SPH, 4096), in_=self.apc(self.F_WGU + par * 2 * KD * 128, 4096),
                        func=AF.Copy), inc=ch)
                    w.wait("vector", ch, b["ch"] + hc + 1)
                    w.op("vector", lambda e, par=par: e.tensor_sub(
                        self.bapc(SPL, 4096), self.apc(self.F_WGU + par * 2 * KD * 128, 4096),
                        self.bapc(SPH, 4096)), inc=cl)
                    w.drain("vector")
                w.wait("scalar", mmg, b["mg"] + hc + 1)
                if hc >= 2:
                    w.wait("scalar", gtd, b["gt"] + hc - 1)
                ssl = self.apc(self.SQT + (hc % 2) * 512, C)
                w.op("scalar", lambda e, ssl=ssl, gb=gb, C=C: e.activation(out=ssl, in_=self.pb(gb, C), func=AF.Silu), inc=sil)
                w.wait("vector", mmu, b["mu"] + hc + 1)
                w.wait("vector", sil, b["si"] + hc + 1)
                if mode == "bf16" and BF16_DOWN:
                    gt = self.apc(self.GTMP, C)
                    w.op("vector", lambda e, gt=gt, ssl=ssl, ub=ub, C=C: e.tensor_tensor(gt, ssl, self.pb(ub, C), ALU.mult))
                    w.drain("vector")
                    w.op("vector", lambda e, j=j, gt=gt, C=C: e.tensor_copy(
                        self.bapc(2 * self.F_GATED + j * 1024, C), gt))
                    w.drain("vector")
                    w.op("vector", lambda e, j=j, gt=gt, C=C: e.tensor_sub(
                        self.bapc(2 * self.F_GATED + j * 1024 + 512, C), gt,
                        self.bapc(2 * self.F_GATED + j * 1024, C)), inc=gtd)
                    w.drain("vector")
                else:
                    gsl = self.apc(self.F_GATED + j * 512, C)
                    w.op("vector", lambda e, gsl=gsl, ssl=ssl, ub=ub, C=C: e.tensor_tensor(gsl, ssl, self.pb(ub, C), ALU.mult), inc=gtd)
                    if j == grp - 1:
                        w.drain("vector")
            if mode == "split" and BF16_DOWN:
                SP2 = 2 * self.RING2 + 4 * KD * 128
                for j in range(grp):
                    hc = gbase[g] + j
                    w.wait("scalar", wdl, b["wd"] + 16 * (g + 1))
                    w.wait("scalar", s2, max(0, b["s2"] + 16 * hc))
                    w.op("scalar", lambda e, j=j: e.activation(
                        out=self.bapc(SP2, D), in_=self.apc(self.F_WD + j * D, D),
                        func=AF.Copy), inc=h2)
                    w.wait("vector", h2, b["h2"] + hc + 1)
                    w.op("vector", lambda e, j=j: e.tensor_sub(
                        self.bapc(SP2 + D, D), self.apc(self.F_WD + j * D, D),
                        self.bapc(SP2, D)), inc=c2)
                    w.drain("vector")
            w.wait("tensor", wdl, b["wd"] + 16 * (g + 1))
            w.wait("tensor", gtd, b["gt"] + gbase[g] + grp)
            for dch in range(KD):
                dpar = dch % 2
                idx = g * KD + dch
                if idx >= 2:
                    w.wait("tensor", acc, b["ac"] + idx - 1)
                if mode == "bf16" and BF16_DOWN:
                    for j in range(grp):
                        wh = self.bapc(2 * self.F_WD + j * 2 * D + dch * 128, 128)
                        wl = self.bapc(2 * self.F_WD + j * 2 * D + D + dch * 128, 128)
                        gh = self.bapc(2 * self.F_GATED + j * 1024, C)
                        gl = self.bapc(2 * self.F_GATED + j * 1024 + 512, C)
                        w.mm(self.pb(4 + dpar, C), wh, gh, j == 0, False)
                        w.mm(self.pb(4 + dpar, C), wh, gl, False, False)
                        w.mm(self.pb(4 + dpar, C), wl, gh, False, j == grp - 1,
                             inc=(dwn if j == grp - 1 else None))
                else:
                    for j in range(grp):
                        w.mm(self.pb(4 + dpar, C), self.apc(self.F_WD + j * D + dch * 128, 128),
                             self.apc(self.F_GATED + j * 512, C), j == 0, j == grp - 1,
                             inc=(dwn if j == grp - 1 else None))
                w.wait("vector", dwn, b["dn"] + idx + 1)
                w.op("vector", lambda e, dch=dch, dpar=dpar, C=C: e.scalar_tensor_tensor(
                    src(dch), self.pb(4 + dpar, C), half, src(dch), ALU.mult, ALU.add), inc=acc)
                if dch == KD - 1:
                    w.drain("vector")

    def emit_ffn_phase(self, w, tiles, dmas, modes=None):
        xld = self.sem("Axl")
        bx = xld.n
        for dst, srcap in dmas:
            w.dma(dst, srcap, inc=xld)
        for e in ("tensor", "scalar", "vector"):
            w.wait(e, xld, bx + 16 * len(dmas))
        for i, (src, C) in enumerate(tiles):
            m = modes[i] if modes else "f32"
            self.emit_ffn_tile(w, src, C, mode=m)

    def emit_qkv(self, w, st):
        ST, KV = self.ST, self.KV
        tag = "B"
        names = ("l0", "l1", "cd", "cr", "pj", "rv", "vr", "td", "tv", "rb", "cl", "rp")
        sms = {s: self.sem(tag + s) for s in names}
        b = self.snap(tag, names)
        csd, csr, pjd, rvd, vrd, trd, tvd, rbd, csl, rpd = (sms[s] for s in names[2:])
        nvb = 0

        if st == 0:
            ranges = [(lambda k, C: self.ring1(k, 0, C), WIN, 0, 0, False, False),
                      (lambda k, C: self.ring1(k, WIN, C), self.ST, WIN, WIN, True, False),
                      (lambda k, C: self.ring2(k, self.ST, C), SINK, self.EXT1 + self.ST, None, False, True)]
        else:
            ranges = [(lambda k, C: self.ring1(k, WIN, C), self.ST, WIN, 0, False, False),
                      (lambda k, C: self.ring2(k, 0, C), self.ST, self.EXT1, WIN, True, False)]

        n_ec = 0
        for ri, (srcf, C, toff, kvc, isq, sink) in enumerate(ranges):
            src = lambda k, srcf=srcf, C=C: srcf(k, C)
            w.wait("vector", pjd)
            bcs, mrow, negm, rrow = self.ln_stats(w, tag + "l", src, C, bcast_m=False)
            w.wait("vector", bcs)
            w.op("vector", lambda e, C=C: e.tensor_copy(self.apc(self.RSB, C), self.pb(5, C)), inc=rbd)
            w.drain("vector")
            w.wait("sync", rpd)
            w.dma(self.apc(self.CS, C), self.cos2[:, toff:toff + C], inc=csl)
            w.dma(self.apc(self.CS + 512, C), self.sin2[:, toff:toff + C], inc=csl)

            projs = [("k", self.wk, NGRP, NGRP * DH, 0, True),
                     ("v", self.wv, NGRP, NGRP * DH, 512, False)]
            if isq:
                projs.append(("q", self.wq, KD, D, 1024, True))
            for pname, wsrc, nec, wcols, sumoff, rope in projs:
                for ec in range(nec):
                    par = n_ec % 2
                    wbase = self.P_WS + par * KD * 128
                    w.wait("sync", pjd, max(0, b["pj"] + n_ec - 1))
                    w.dma(self.ap3(wbase, 128, KD, 1, 128),
                          bass.AP(wsrc.tensor, ec * 128, [[wcols, 128], [128 * wcols, KD], [1, 128]]), inc=sms["l%d" % par])
                    w.wait("tensor", sms["l%d" % par], b["l%d" % par] + 16 * (n_ec // 2 + 1))
                    sumrow = self.apc(self.P_SUM + sumoff + (0 if sumoff == 1024 else ec * 128), 128, parts=1)
                    need_sums = (ri == 0) or pname == "q"
                    if need_sums:
                        w.wait("tensor", csr)
                        for q4 in range(4):
                            w.mm(self.pb(4, 512, parts=1), self.apc(self.ONESC, 1),
                                 self.apc(wbase + q4 * 512, 512), q4 == 0, q4 == 3,
                                 inc=(csd if q4 == 3 else None))
                        w.wait("vector", csd)
                        w.op("vector", lambda e, sumrow=sumrow: e.tensor_reduce(
                            sumrow, bass.AP(self.ps, 4 * 512, [[4096, 1], [1, 128], [128, 4]]),
                            AX.X, ALU.add), inc=csr)
                        w.drain("vector")
                        w.wait("tensor", csr)
                    if n_ec >= 2:
                        w.wait("tensor", rvd, b["rv"] + n_ec - 1)
                    for k in range(KD):
                        w.mm(self.pb(par, C), self.apc(wbase + k * 128, 128), src(k), k == 0, False)
                    w.wait("tensor", rbd)
                    w.mm(self.pb(par, C), sumrow, negm, False, True, inc=pjd)
                    w.wait("vector", pjd, b["pj"] + n_ec + 1)
                    w.wait("vector", csl)
                    bcol = {"k": self.apc(self.BIASC + 2 * KD + ec, 1),
                            "v": self.apc(self.BIASC + 2 * KD + NGRP + ec, 1),
                            "q": self.apc(self.BIASC + ec, 1)}[pname]
                    rt0 = self.apc(self.RT, C)
                    rt1 = self.apc(self.RT + 512, C)
                    rt2 = self.apc(self.RT + 1024, C)
                    w.wait("vector", trd)
                    w.op("vector", lambda e, par=par, C=C: e.tensor_tensor(rt0, self.pb(par, C), self.apc(self.RSB, C), ALU.mult), inc=rvd)
                    w.drain("vector")
                    if pname == "q":
                        w.op("vector", lambda e, bcol=bcol, rt0=rt0: e.tensor_scalar(rt0, rt0, bcol, SCALE, ALU.add, ALU.mult), inc=vrd)
                    else:
                        w.op("vector", lambda e, bcol=bcol, rt0=rt0: e.tensor_scalar_add(rt0, rt0, bcol), inc=vrd)
                    w.drain("vector")
                    if pname == "q":
                        dst = self.apc(self.P_Q + ec * ST, C)
                    elif pname == "k":
                        dst = self.apc(self.KSINK + ec * SINK, SINK) if sink else self.apc(self.P_K + ec * KV + kvc, C)
                    else:
                        dst = None
                    if rope:
                        w.op("vector", lambda e, rt1=rt1, rt0=rt0: e.stream_shuffle(rt1, rt0, SWAP_ADJ))
                        w.drain("vector")
                        w.op("vector", lambda e, rt2=rt2, rt0=rt0, C=C: e.tensor_tensor(rt2, rt0, self.apc(self.CS, C), ALU.mult))
                        w.op("vector", lambda e, rt1=rt1, C=C: e.tensor_tensor(rt1, rt1, self.apc(self.CS + 512, C), ALU.mult))
                        w.drain("vector")
                        w.op("vector", lambda e, dst=dst, rt2=rt2, rt1=rt1: e.tensor_add(dst, rt2, rt1), inc=rpd)
                        w.drain("vector")
                    else:
                        nblk = (C + 127) // 128
                        for jj in range(nblk):
                            cc = min(128, C - jj * 128)
                            tb = 2 + nvb % 2
                            w.wait("tensor", vrd)
                            if nvb >= 2:
                                w.wait("tensor", tvd, b["tv"] + nvb - 1)
                            w.tr(self.pb(tb, 128, parts=cc),
                                 self.apc(self.RT + jj * 128, cc), self.apc(self.IDENT, 128), inc=trd)
                            if sink:
                                vdst = self.apc(self.VSINK + ec * 128, 128, parts=cc)
                            else:
                                vdst = self.apc(self.P_V + (kvc // 128 + jj) * 512 + ec * 128, 128, parts=cc)
                            w.wait("scalar", trd)
                            w.op("scalar", lambda e, vdst=vdst, tb=tb, cc=cc: e.activation(
                                out=vdst, in_=self.pb(tb, 128, parts=cc), func=AF.Copy), inc=tvd)
                            nvb += 1
                    n_ec += 1

    def emit_attn(self, w, st):
        ST, KV, NQT, MCOL = self.ST, self.KV, self.NQT, self.MCOL
        tag = "C"
        names = ("ml", "qk", "mk", "ex", "sm", "pn", "pt", "pe", "pv", "av", "o0", "o1", "oo", "xb")
        sms = {s: self.sem(tag + s) for s in names}
        b = self.snap(tag, names)
        mld, qkd, mkd, exd, smd, pnd, ptd, pte, pvd, ave = (sms[s] for s in names[:10])
        ood, xbd = sms["oo"], sms["xb"]

        nmx = self.apc(self.NMX, 1)
        ssm = self.apc(self.NMX + 1, 1)
        rcp = self.apc(self.NMX + 2, 1)
        NIT = NQT * NHEAD

        def emit_qk(it):
            i, h = divmod(it, NHEAD)
            g = h // GSZ
            p = it % 2
            if i > 0 and h == 0:
                w.wait("sync", mkd, b["mk"] + i * NHEAD)
                w.dma(self.apc(self.P_MASK, MCOL),
                      bass.AP(self.maskd.tensor, (st * NQT + i) * 128 * MCOL, [[MCOL, 128], [1, MCOL]]),
                      inc=mld)
            if it >= 2:
                w.wait("tensor", exd, b["ex"] + it - 1)
            qsl = self.apc(self.P_Q + h * ST + i * 128, 128)
            w.mm(self.pb(2 * p, 512), qsl, self.apc(self.P_K + g * KV + i * 128, 512), True, True)
            w.mm(self.pb(2 * p + 1, 128), qsl, self.apc(self.P_K + g * KV + i * 128 + 512, 128), True, False)
            w.mm(self.pb(2 * p + 1, SINK, c0=128), qsl, self.apc(self.KSINK + g * SINK, SINK), False, True, inc=qkd)

        def emit_softmax(it):
            p = it % 2
            w.wait("vector", qkd, b["qk"] + it + 1)
            w.wait("vector", mld)
            if it >= 1:
                w.wait("vector", pnd, b["pn"] + it)
            w.op("vector", lambda e, p=p: e.tensor_tensor(self.psp(p * 1024, 644), self.psp(p * 1024, 644),
                                                          self.apc(self.P_MASK, 644), ALU.add))
            w.drain("vector")
            w.op("vector", lambda e, p=p: e.tensor_reduce(nmx, self.psp(p * 1024, 644), AX.X, ALU.max, negate=True), inc=mkd)
            w.wait("scalar", mkd, b["mk"] + it + 1)
            w.op("scalar", lambda e, p=p: e.activation(out=self.apc(self.P_P, 644), in_=self.psp(p * 1024, 644),
                                                       func=AF.Exp, bias=nmx, scale=1.0, accum_out=ssm), inc=exd)
            w.wait("vector", exd, b["ex"] + it + 1)
            w.op("vector", lambda e: e.reciprocal(rcp, ssm), inc=smd)
            w.wait("scalar", smd, b["sm"] + it + 1)
            w.wait("scalar", ptd, b["pt"] + it)
            w.op("scalar", lambda e: e.activation(out=self.apc(self.P_PN, 644), in_=self.apc(self.P_P, 644),
                                                  func=AF.Copy, scale=rcp), inc=pnd)

        def emit_pv(it):
            i, h = divmod(it, NHEAD)
            g = h // GSZ
            w.wait("tensor", pnd, b["pn"] + it + 1)
            if it >= 1:
                w.wait("tensor", pte, b["pe"] + it)
            for j in range(4):
                w.tr(self.pb(4, 128, c0=j * 128), self.apc(self.P_PN + j * 128, 128), self.apc(self.IDENT, 128))
            w.tr(self.pb(5, 128), self.apc(self.P_PN + 512, 128), self.apc(self.IDENT, 128))
            w.tr(self.pb(5, 128, c0=128, parts=SINK), self.apc(self.P_PN + 640, SINK), self.apc(self.IDENT, 128), inc=ptd)
            w.wait("scalar", ptd, b["pt"] + it + 1)
            w.op("scalar", lambda e: e.activation(out=self.apc(self.P_PT, 512), in_=self.pb(4, 512), func=AF.Copy))
            w.op("scalar", lambda e: e.activation(out=self.apc(self.P_PT + 512, 256), in_=self.pb(5, 256), func=AF.Copy), inc=pte)
            w.drain("scalar")
            par = 6 + it % 2
            w.wait("tensor", pte, b["pe"] + it + 1)
            if it >= 2:
                w.wait("tensor", ave, b["av"] + it - 1)
            for j in range(5):
                w.mm(self.pb(par, 128), self.apc(self.P_V + (i + j) * 512 + g * 128, 128),
                     self.apc(self.P_PT + j * 128, 128), j == 0, False)
            w.mm(self.pb(par, 128), self.apc(self.VSINK + g * 128, 128, parts=SINK),
                 self.apc(self.P_PT + 640, 128, parts=SINK), False, True, inc=pvd)
            w.wait("scalar", pvd, b["pv"] + it + 1)
            w.op("scalar", lambda e, h=h, i=i, par=par: e.activation(
                out=self.apc(self.P_Q + h * ST + i * 128, 128), in_=self.pb(par, 128), func=AF.Copy), inc=ave)

        w.dma(self.apc(self.P_MASK, MCOL),
              bass.AP(self.maskd.tensor, (st * NQT) * 128 * MCOL, [[MCOL, 128], [1, MCOL]]),
              inc=mld)
        for it in range(NIT + 1):
            if it < NIT:
                emit_qk(it)
                emit_softmax(it)
            if it >= 1:
                emit_pv(it - 1)

        res = (lambda k, c0, c: self.ring1(k, WIN + c0, c)) if st == 0 else (lambda k, c0, c: self.ring2(k, c0, c))
        xb = (lambda k, c0, c: self.ring1(k, c0, c)) if st == 0 else (lambda k, c0, c: self.ring1(k, WIN + c0, c))
        for ec in range(KD):
            par = 6 + ec % 2
            wbase = self.P_WS + (ec % 2) * KD * 128
            w.wait("sync", ood, max(0, b["oo"] + ec - 1))
            w.dma(self.ap3(wbase, 128, KD, 1, 128),
                  bass.AP(self.wo.tensor, ec * 128, [[D, 128], [128 * D, KD], [1, 128]]), inc=sms["o%d" % (ec % 2)])
            w.wait("tensor", sms["o%d" % (ec % 2)], b["o%d" % (ec % 2)] + 16 * (ec // 2 + 1))
            if ec == 0:
                w.wait("tensor", ave, b["av"] + NQT * NHEAD)
            if ec >= 2:
                w.wait("tensor", xbd, b["xb"] + ec - 1)
            for cc in range(KD):
                w.mm(self.pb(par, ST), self.apc(wbase + cc * 128, 128),
                     self.apc(self.P_Q + cc * ST, ST),
                     cc == 0, cc == KD - 1, inc=(ood if cc == KD - 1 else None))
            w.wait("vector", ood, b["oo"] + ec + 1)
            for ct in range(NQT):
                w.op("vector", lambda e, ec=ec, ct=ct, par=par: e.tensor_add(
                    xb(ec, ct * 128, 128), res(ec, ct * 128, 128), self.pb(par, 128, c0=ct * 128)))
            w.drain("vector")
            w.op("vector", lambda e, ec=ec, xb=xb, ST=ST: e.tensor_scalar_add(
                xb(ec, 0, ST), xb(ec, 0, ST), self.apc(self.BIASC + KD + ec, 1)), inc=xbd)
            w.drain("vector")

    def emit_out(self, w):
        so = self.sem("out")
        bo = so.n
        w.dma(bass.AP(self.yt.tensor, 0, [[self.OWN, 128], [128 * self.OWN, KD], [1, self.ST]]),
              self.ap3(self.RING1, self.EXT1, KD, 1, self.ST), inc=so)
        w.dma(bass.AP(self.yt.tensor, self.ST, [[self.OWN, 128], [128 * self.OWN, KD], [1, self.ST]]),
              self.ap3(self.RING1 + WIN, self.EXT1, KD, 1, self.ST), inc=so)
        w.wait("sync", so, bo + 32)

    def build(self):
        ST = self.ST
        self.phase("pre", self.emit_preload)
        a1_dma = [(self.ap3(self.RING1, self.EXT1, KD, 1, self.EXT1),
                   bass.AP(self.xe.tensor, 0, [[self.EXT1, 128], [128 * self.EXT1, KD], [1, self.EXT1]]))]
        a1_tiles = [((lambda k, c0=c0, C=C: self.ring1(k, c0, C)), C) for c0, C in balanced_split(self.EXT1)]
        a1_modes = (["split"] + ["bf16"] * (len(a1_tiles) - 1)) if BF16_GU else None
        self.phase("A1", lambda w: self.emit_ffn_phase(w, a1_tiles, a1_dma, a1_modes))
        a2_dma = [(self.ap3(self.RING2, self.R2, KD, 1, ST),
                   bass.AP(self.xo2.tensor, 0, [[ST, 128], [128 * ST, KD], [1, ST]])),
                  (self.ap3(self.RING2 + ST, self.R2, KD, 1, SINK),
                   bass.AP(self.xsink.tensor, 0, [[SINK, 128], [128 * SINK, KD], [1, SINK]]))]
        a2_tiles = [((lambda k, c0=c0, C=C: self.ring2(k, c0, C)), C) for c0, C in balanced_split(self.R2)]
        bfm = lambda ts: (["bf16"] * len(ts)) if BF16_GU else None
        self.phase("A2", lambda w: self.emit_ffn_phase(w, a2_tiles, a2_dma, bfm(a2_tiles)))
        self.phase("B1", lambda w: self.emit_qkv(w, 0))
        self.phase("C1", lambda w: self.emit_attn(w, 0))
        d1_tiles = [((lambda k, c0=c0, C=C: self.ring1(k, c0, C)), C) for c0, C in balanced_split(ST)]
        self.phase("D1", lambda w: self.emit_ffn_phase(w, d1_tiles, [], bfm(d1_tiles)))
        self.phase("B2", lambda w: self.emit_qkv(w, 1))
        self.phase("C2", lambda w: self.emit_attn(w, 1))
        d2_tiles = [((lambda k, c0=c0, C=C: self.ring1(k, WIN + c0, C)), C) for c0, C in balanced_split(ST)]
        self.phase("D2", lambda w: self.emit_ffn_phase(w, d2_tiles, [], bfm(d2_tiles)))
        self.phase("out", self.emit_out)
        return self.nc



def rope_tables(positions):
    half = 64
    inv = 1.0 / (10000.0 ** (np.arange(half, dtype=np.float32) * 2.0 / DH))
    ang = positions[None, :].astype(np.float32) * inv[:, None]
    c, s = np.cos(ang), np.sin(ang)
    cos2 = np.empty((128, ang.shape[1]), np.float32)
    sin2 = np.empty((128, ang.shape[1]), np.float32)
    cos2[0::2] = c
    cos2[1::2] = c
    sin2[0::2] = -s
    sin2[1::2] = s
    return cos2, sin2


def perm_cols():
    p = np.empty(DH, np.int64)
    p[0::2] = np.arange(64)
    p[1::2] = np.arange(64) + 64
    return p


def build_masks(own, p0):
    STl = own // 2
    NQT = STl // 128
    MCOL = WIN + 128 + SINK
    out = np.zeros((2 * NQT, 128, MCOL), np.float32)
    for st in range(2):
        for i in range(NQT):
            q0g = p0 + st * STl + i * 128
            rows = q0g + np.arange(128)[:, None]
            kw = (q0g - WIN) + np.arange(WIN + 128)[None, :]
            valid = (kw >= 0) & (kw <= rows) & (kw >= rows - WIN)
            m = out[st * NQT + i]
            m[:, :WIN + 128] = np.where(valid, 0.0, NEG)
            ks = np.arange(SINK)[None, :]
            vs = (ks <= rows) & (rows - ks > WIN)
            m[:, WIN + 128:] = np.where(vs, 0.0, NEG)
    return out


_BUILD_CACHE = {}


def get_nc(own):
    if own not in _BUILD_CACHE:
        _BUILD_CACHE[own] = Builder(own).build()
    return _BUILD_CACHE[own]


def make_core_inputs(xt, weights, own, b, p0, ttot):
    T = xt.shape[2]
    ext0 = p0 - WIN
    xe = np.zeros((D, WIN + own // 2), np.float32)
    lo = max(0, ext0)
    xe[:, lo - ext0:] = xt[b, :, lo:p0 + own // 2]
    d = dict(weights)
    d["xe"] = xe
    d["xo2"] = np.ascontiguousarray(xt[b, :, p0 + own // 2:p0 + own])
    d["xsink"] = np.ascontiguousarray(xt[b, :, 0:SINK])
    pos = np.concatenate([np.maximum(np.arange(ext0, p0 + own // 2), 0),
                          np.arange(p0 + own // 2, p0 + own),
                          np.arange(SINK)]).astype(np.float32)
    assert pos.shape[0] == ttot
    c2, s2 = rope_tables(pos)
    d["cos2"] = c2
    d["sin2"] = s2
    d["maskd"] = build_masks(own, p0).reshape(-1, WIN + 128 + SINK)
    return d


def prep_weights(L0_attn_Wq, L0_attn_bq, L0_attn_Wk, L0_attn_bk, L0_attn_Wv,
                 L0_attn_bv, L0_attn_Wo, L0_attn_bo, L0_ffn_w_gate,
                 L0_ffn_w_up, L0_ffn_w_down):
    pc = perm_cols()
    qperm = np.concatenate([h * DH + pc for h in range(NHEAD)])
    kperm = np.concatenate([g * DH + pc for g in range(NGRP)])
    w = {}
    w["wgt"] = np.ascontiguousarray(np.asarray(L0_ffn_w_gate).T)
    w["wut"] = np.ascontiguousarray(np.asarray(L0_ffn_w_up).T)
    w["wdt"] = np.ascontiguousarray(np.asarray(L0_ffn_w_down).T)
    w["wq"] = np.ascontiguousarray(np.asarray(L0_attn_Wq)[:, qperm])
    w["wk"] = np.ascontiguousarray(np.asarray(L0_attn_Wk)[:, kperm])
    w["wv"] = np.ascontiguousarray(np.asarray(L0_attn_Wv))
    w["wo"] = np.ascontiguousarray(np.asarray(L0_attn_Wo))
    w["bqc"] = np.ascontiguousarray(np.asarray(L0_attn_bq)[qperm].reshape(KD, 128).T)
    w["bkc"] = np.ascontiguousarray(np.asarray(L0_attn_bk)[kperm].reshape(NGRP, 128).T)
    w["bvc"] = np.ascontiguousarray(np.asarray(L0_attn_bv).reshape(NGRP, 128).T)
    w["boc"] = np.ascontiguousarray(np.asarray(L0_attn_bo).reshape(KD, 128).T)
    return w


def kernel(L0_attn_x, L0_attn_Wq, L0_attn_bq, L0_attn_Wk, L0_attn_bk,
           L0_attn_Wv, L0_attn_bv, L0_attn_Wo, L0_attn_bo, L0_ffn_w_gate,
           L0_ffn_w_up, L0_ffn_w_down):
    x = np.asarray(L0_attn_x, dtype=np.float32)
    B, T, _ = x.shape
    n_cores = 8
    chunks = n_cores // B
    own = T // chunks
    xt = np.ascontiguousarray(x.transpose(0, 2, 1))
    weights = prep_weights(L0_attn_Wq, L0_attn_bq, L0_attn_Wk, L0_attn_bk,
                           L0_attn_Wv, L0_attn_bv, L0_attn_Wo, L0_attn_bo,
                           L0_ffn_w_gate, L0_ffn_w_up, L0_ffn_w_down)
    ttot = WIN + own + SINK
    in_maps = []
    for c in range(n_cores):
        b, j = c // chunks, c % chunks
        in_maps.append(make_core_inputs(xt, weights, own, b, j * own, ttot))
    nc = get_nc(own)
    res = run_bass_kernel_spmd(nc, in_maps, core_ids=list(range(n_cores)))
    yt = np.empty((B, D, T), np.float32)
    for c in range(n_cores):
        b, j = c // chunks, c % chunks
        yt[b, :, j * own:(j + 1) * own] = res.results[c]["yt"]
    return np.ascontiguousarray(yt.transpose(0, 2, 1))


# version 15 (final)
# speedup vs baseline: 1.0282x; 1.0282x over previous
import numpy as np
import concourse.bass as bass
import concourse.mybir as mybir
from concourse.bass_utils import run_bass_kernel_spmd

F32 = mybir.dt.float32
AF = mybir.ActivationFunctionType
ALU = mybir.AluOpType
AX = mybir.AxisListType

D = 2048
HID = 8192
KD = D // 128
KH = HID // 128
NHEAD = 16
NGRP = 4
GSZ = NHEAD // NGRP
DH = 128
WIN = 512
SINK = 4
EPS = 1e-5
SCALE = float(DH) ** -0.5
NEG = -1.0e30
FFN_GROUPS = [3] * 21 + [1]
GRP_MAX = 3
BF16_GU = True
BF16_DOWN = True

ENGS = ("sync", "tensor", "scalar", "vector")
SWAP_ADJ = [i ^ 1 for i in range(32)]


class Sem:
    __slots__ = ("h", "n")

    def __init__(self, nc, name):
        self.h = nc.alloc_semaphore(name=name)
        self.n = 0


class W:

    def __init__(self, eng_name, eng):
        self.e = eng_name
        self.eng = eng

    def mm(self, out, lhsT, rhs, start, stop, inc=None):
        if self.e == "tensor":
            ins = self.eng.matmul(out, lhsT, rhs, start=start, stop=stop)
            if inc is not None:
                ins.then_inc(inc.h, 1)
        if inc is not None:
            inc.n += 1

    def tr(self, out, in_, ident, inc=None):
        if self.e == "tensor":
            ins = self.eng.transpose(out, in_, ident)
            if inc is not None:
                ins.then_inc(inc.h, 1)
        if inc is not None:
            inc.n += 1

    def op(self, engine, fn, inc=None, inck=1):
        if self.e == engine:
            ins = fn(self.eng)
            if inc is not None:
                ins.then_inc(inc.h, inck)
        if inc is not None:
            inc.n += inck

    def dma(self, out, in_, inc=None):
        if self.e == "sync":
            ins = self.eng.dma_start(out=out, in_=in_)
            if inc is not None:
                ins.then_inc(inc.h, 16)
        if inc is not None:
            inc.n += 16

    def wait(self, engine, sem, val=None):
        if self.e == engine:
            self.eng.wait_ge(sem.h, sem.n if val is None else val)

    def drain(self, engine):
        if self.e == engine:
            self.eng.drain()


def balanced_split(total, cap=512):
    n = -(-total // cap)
    base = -(-total // n)
    out, o = [], 0
    while o < total:
        c = min(base, total - o)
        out.append((o, c))
        o += c
    return out


class Builder:
    def __init__(self, own):
        self.OWN = own
        self.ST = own // 2
        self.EXT1 = WIN + self.ST
        self.R2 = self.ST + SINK
        self.NQT = self.ST // 128
        self.KV = WIN + self.ST
        self.TTOT = self.EXT1 + self.ST + SINK
        self.MCOL = WIN + 128 + SINK
        nc = bass.Bass("TRN2", target_bir_lowering=False)
        self.nc = nc

        def t(n, s):
            return nc.dram_tensor(n, s, F32, kind="ExternalInput").ap()

        self.xe = t("xe", [D, self.EXT1])
        self.xo2 = t("xo2", [D, self.ST])
        self.xsink = t("xsink", [D, SINK])
        self.wgt = t("wgt", [D, HID])
        self.wut = t("wut", [D, HID])
        self.wdt = t("wdt", [HID, D])
        self.wq = t("wq", [D, D])
        self.wk = t("wk", [D, NGRP * DH])
        self.wv = t("wv", [D, NGRP * DH])
        self.wo = t("wo", [D, D])
        self.bqc = t("bqc", [128, KD])
        self.bkc = t("bkc", [128, NGRP])
        self.bvc = t("bvc", [128, NGRP])
        self.boc = t("boc", [128, KD])
        self.cos2 = t("cos2", [128, self.TTOT])
        self.sin2 = t("sin2", [128, self.TTOT])
        self.maskd = t("maskd", [2 * self.NQT * 128, self.MCOL])
        self.yt = nc.dram_tensor("yt", [D, self.OWN], F32, kind="ExternalOutput").ap()

        self.ACOLS = 52600
        self.sb = nc.alloc_sbuf_tensor("arena", [128, self.ACOLS], F32)
        o = [0]

        def region(n):
            r = o[0]
            o[0] += n
            return r

        self.RING1 = region(KD * self.EXT1)
        self.RING2 = region(KD * self.R2)
        PH = region(24832)
        self.PHASE = PH
        self.IDENT = region(128)
        self.ONESC = region(1)
        self.ONESR = region(128)
        self.BIASC = region(2 * KD + 2 * NGRP)
        self.KSINK = region(NGRP * SINK)
        self.VSINK = region(NGRP * DH)
        self.SROW = region(2 * 528)
        self.SQT = region(1024)
        self.NMX = region(4)
        self.GT2 = self.SQT
        assert o[0] <= self.ACOLS, o[0]

        self.F_XN = PH
        self.F_GATED = PH + KD * 512
        self.F_WGU = self.F_GATED + GRP_MAX * 512
        self.F_WD = self.F_WGU + 2 * (2 * KD * 128)
        assert self.F_WD + GRP_MAX * D - PH <= 24832
        self.P_Q = PH
        self.P_K = self.P_Q + KD * self.ST
        self.P_V = self.P_K + NGRP * self.KV
        self.P_WS = self.P_V + (self.KV // 128) * 512
        self.P_SUM = self.P_WS + 2 * KD * 128
        SCR = self.P_SUM + 1152
        self.CS = SCR
        self.RSB = SCR + 1024
        self.RT = SCR + 1536
        self.P_MASK = SCR
        self.P_P = SCR + 648
        self.P_PN = SCR + 1296
        self.P_PT = SCR + 1944
        assert SCR + 3072 - PH <= 24832, SCR + 3072 - PH

        self.ps = nc.alloc_psum_tensor("psarena", [128, 4096], F32)
        self.bb = self.sb.bitcast(mybir.dt.bfloat16)
        self.sc_wgu = nc.dram_tensor("sc_wgu", [KH, 128, 4 * KD * 128],
                                     mybir.dt.bfloat16, kind="Internal")
        self.sc_wd = nc.dram_tensor("sc_wd", [KH, 128, 2 * D],
                                    mybir.dt.bfloat16, kind="Internal")
        self.GTMP = self.PHASE + 24064
        self.sems = {}
        self._consts = {}

    def apc(self, base, cols, part0=0, parts=128):
        return self.sb.ap()[part0:part0 + parts, base:base + cols]

    def ap3(self, base, d1, n1, d2, n2):
        return bass.AP(self.sb, base, [[self.ACOLS, 128], [d1, n1], [d2, n2]])

    def bapc(self, bcol, cols, part0=0, parts=128):
        return self.bb.ap()[part0:part0 + parts, bcol:bcol + cols]

    def bp3(self, bcol, d1, n1, d2, n2):
        return bass.AP(self.bb, bcol, [[2 * self.ACOLS, 128], [d1, n1], [d2, n2]])

    def pb(self, bank, cols=512, c0=0, parts=128):
        return self.ps.ap()[0:parts, bank * 512 + c0:bank * 512 + c0 + cols]

    def psp(self, c0, cols):
        return self.ps.ap()[:, c0:c0 + cols]

    def ring1(self, k, c0, c):
        return self.apc(self.RING1 + k * self.EXT1 + c0, c)

    def ring2(self, k, c0, c):
        return self.apc(self.RING2 + k * self.R2 + c0, c)

    def sem(self, name):
        if name not in self.sems:
            self.sems[name] = Sem(self.nc, name)
        return self.sems[name]

    def snap(self, tag, names):
        return {s: self.sem(tag + s).n for s in names}

    def const(self, name, arr):
        if name not in self._consts:
            self._consts[name] = self.nc.inline_tensor(np.ascontiguousarray(arr), name)
        return self._consts[name]

    def phase(self, name, emit):
        presnap = {k: s.n for k, s in self.sems.items()}
        counts = None
        with self.nc.Block(name) as blk:
            for e in ENGS:
                for k, s in self.sems.items():
                    s.n = presnap.get(k, 0)
                w = W(e, getattr(self.nc, e))
                getattr(blk, e)(lambda eng, w=w: emit(w))
                chk = {k: s.n for k, s in self.sems.items()}
                if counts is None:
                    counts = chk
                else:
                    assert chk == counts, f"walk divergence in {name}"

    def emit_preload(self, w):
        sl = self.sem("pre")
        w.dma(self.apc(self.IDENT, 128), self.const("identc", np.eye(128, dtype=np.float32)).ap(), inc=sl)
        w.dma(self.apc(self.ONESC, 1), self.const("onescc", np.ones((128, 1), np.float32)).ap(), inc=sl)
        w.dma(self.apc(self.ONESR, 128, parts=1), self.const("onesrc", np.ones((1, 128), np.float32)).ap(), inc=sl)
        w.dma(self.apc(self.BIASC, KD), self.bqc, inc=sl)
        w.dma(self.apc(self.BIASC + KD, KD), self.boc, inc=sl)
        w.dma(self.apc(self.BIASC + 2 * KD, NGRP), self.bkc, inc=sl)
        w.dma(self.apc(self.BIASC + 2 * KD + NGRP, NGRP), self.bvc, inc=sl)
        for e in ("tensor", "scalar", "vector"):
            w.wait(e, sl)

    def ln_stats(self, w, tag, src, C, bcast_m=True):
        sq = self.sem(tag + "sq")
        st = self.sem(tag + "st")
        vr = self.sem(tag + "vr")
        rs = self.sem(tag + "rs")
        bc = self.sem(tag + "bc")
        b = self.snap(tag, ("sq", "st", "vr", "rs", "bc"))
        mrow = self.apc(self.SROW, C, parts=1)
        negm = mrow if not bcast_m else None
        rrow = self.apc(self.SROW + 528, C, parts=1)
        t2 = self.apc(self.SQT, C, parts=1)
        w.wait("vector", bc, b["bc"])
        w.wait("tensor", vr, b["vr"])
        w.wait("tensor", self.sem("Aac"), self.sem("Aac").n)
        for k in range(KD):
            tsl = self.apc(self.SQT + (k % 2) * 512, C)
            if k >= 2:
                w.wait("scalar", st, b["st"] + k - 1)
            w.op("scalar", lambda e, tsl=tsl, k=k: e.activation(out=tsl, in_=src(k), func=AF.Square), inc=sq)
            w.wait("tensor", sq, b["sq"] + k + 1)
            w.mm(self.pb(6, C, parts=1), self.apc(self.ONESC, 1), src(k), k == 0, k == KD - 1)
            w.mm(self.pb(7, C, parts=1), self.apc(self.ONESC, 1), tsl, k == 0, k == KD - 1,
                 inc=(st if True else None))
        w.wait("vector", st, b["st"] + KD)
        w.op("vector", lambda e: e.tensor_scalar_mul(mrow, self.pb(6, C, parts=1), 1.0 / D))
        w.op("vector", lambda e: e.tensor_scalar_mul(rrow, self.pb(7, C, parts=1), 1.0 / D))
        w.drain("vector")
        w.op("vector", lambda e: e.tensor_tensor(t2, mrow, mrow, ALU.mult))
        w.drain("vector")
        if not bcast_m:
            w.op("vector", lambda e: e.tensor_scalar_mul(mrow, mrow, -1.0))
        w.op("vector", lambda e: e.tensor_sub(rrow, rrow, t2))
        w.drain("vector")
        w.op("vector", lambda e: e.tensor_scalar_add(rrow, rrow, EPS), inc=vr)
        w.wait("scalar", vr, b["vr"] + 1)
        w.op("scalar", lambda e: e.activation(out=t2, in_=rrow, func=AF.Ln))
        w.drain("scalar")
        w.op("scalar", lambda e: e.activation(out=rrow, in_=t2, func=AF.Exp, scale=-0.5), inc=rs)
        w.wait("tensor", rs, b["rs"] + 1)
        if bcast_m:
            w.mm(self.pb(4, C), self.apc(self.ONESR, 128, parts=1), mrow, True, True)
        w.mm(self.pb(5, C), self.apc(self.ONESR, 128, parts=1), rrow, True, True, inc=bc)
        return bc, mrow, negm, rrow

    def emit_ffn_tile(self, w, src, C, half=0.5, mode="f32"):
        tag = "A"
        names = ("xn", "w0", "w1", "wd", "mg", "mu", "si", "gt", "dn", "ac",
                 "ch", "cl", "so", "xc", "h2", "c2", "s2", "gh", "gl")
        sms = {s: self.sem(tag + s) for s in names}
        b = self.snap(tag, names)
        xnd, wdl = sms["xn"], sms["wd"]
        mmg, mmu, sil, gtd, dwn, acc = (sms[s] for s in ("mg", "mu", "si", "gt", "dn", "ac"))
        ch, cl, so, xc = (sms[s] for s in ("ch", "cl", "so", "xc"))
        h2, c2, s2, ghs, gls = (sms[s] for s in ("h2", "c2", "s2", "gh", "gl"))
        WGUB = 2 * self.F_WGU
        XH = 2 * self.F_XN
        XL = XH + KD * 512
        SPH = 2 * self.RING2
        SPL = SPH + 2 * KD * 128

        NG = len(FFN_GROUPS)
        gbase = [sum(FFN_GROUPS[:i]) for i in range(NG)]
        for g in range(NG):
            grp = FFN_GROUPS[g]
            for j in range(grp):
                hc = gbase[g] + j
                par = hc % 2
                w.wait("sync", mmu, max(0, b["mu"] + hc - 1))
                if mode == "bf16":
                    w.wait("sync", so, 32 * (hc + 1))
                    w.dma(self.bapc(WGUB + par * 8192, 8192),
                          bass.AP(self.sc_wgu, hc * 128 * 8192, [[8192, 128], [1, 8192]]),
                          inc=sms["w%d" % par])
                else:
                    if mode == "split":
                        w.wait("sync", cl, max(0, b["cl"] + hc - 1))
                    w.dma(self.ap3(self.F_WGU + par * 2 * KD * 128, 128, KD, 1, 128),
                          bass.AP(self.wgt.tensor, hc * 128, [[HID, 128], [128 * HID, KD], [1, 128]]), inc=sms["w%d" % par])
                    w.dma(self.ap3(self.F_WGU + par * 2 * KD * 128 + KD * 128, 128, KD, 1, 128),
                          bass.AP(self.wut.tensor, hc * 128, [[HID, 128], [128 * HID, KD], [1, 128]]), inc=sms["w%d" % par])
                if mode == "split":
                    if hc > 0:
                        w.wait("sync", cl, b["cl"] + hc)
                        w.dma(bass.AP(self.sc_wgu, (hc - 1) * 128 * 8192, [[8192, 128], [1, 4096]]),
                              self.bapc(SPH, 4096), inc=so)
                        w.dma(bass.AP(self.sc_wgu, (hc - 1) * 128 * 8192 + 4096, [[8192, 128], [1, 4096]]),
                              self.bapc(SPL, 4096), inc=so)
                    if hc == KH - 1:
                        w.wait("sync", cl, b["cl"] + hc + 1)
                        w.dma(bass.AP(self.sc_wgu, hc * 128 * 8192, [[8192, 128], [1, 4096]]),
                              self.bapc(SPH, 4096), inc=so)
                        w.dma(bass.AP(self.sc_wgu, hc * 128 * 8192 + 4096, [[8192, 128], [1, 4096]]),
                              self.bapc(SPL, 4096), inc=so)
            if mode == "bf16" and BF16_DOWN:
                w.wait("sync", dwn, max(0, b["dn"] + g * KD))
                for j in range(grp):
                    hc = gbase[g] + j
                    w.wait("sync", s2, 16 * (hc + 1))
                    w.dma(self.bapc(2 * self.F_WD + j * 2 * D, 2 * D),
                          bass.AP(self.sc_wd, hc * 128 * 2 * D, [[2 * D, 128], [1, 2 * D]]),
                          inc=wdl)
            else:
                w.wait("sync", dwn, max(0, b["dn"] + g * KD))
                if mode == "split" and BF16_DOWN:
                    w.wait("sync", c2, max(0, b["c2"] + gbase[g]))
                w.dma(self.ap3(self.F_WD, D, grp, 1, D),
                      bass.AP(self.wdt.tensor, gbase[g] * 128 * D, [[D, 128], [128 * D, grp], [1, D]]),
                      inc=wdl)
                if mode == "split" and BF16_DOWN:
                    for j in range(grp):
                        hc = gbase[g] + j
                        w.wait("sync", c2, b["c2"] + hc + 1)
                        w.dma(bass.AP(self.sc_wd, hc * 128 * 2 * D, [[2 * D, 128], [1, 2 * D]]),
                              self.bapc(2 * self.RING2 + 4 * KD * 128, 2 * D), inc=s2)

        w.wait("scalar", gtd, b["gt"])
        w.wait("vector", mmu, b["mu"])

        bc, mrow, negm, rrow = self.ln_stats(w, tag + "l", src, C, bcast_m=True)
        w.wait("vector", bc)
        if mode == "bf16":
            for k in range(KD):
                tmp = self.apc(self.GT2 + (k % 2) * 512, C)
                w.op("vector", lambda e, tmp=tmp, k=k: e.tensor_sub(tmp, src(k), self.pb(4, C)))
                w.drain("vector")
                w.op("vector", lambda e, tmp=tmp: e.tensor_tensor(tmp, tmp, self.pb(5, C), ALU.mult))
                w.drain("vector")
                w.op("vector", lambda e, tmp=tmp, k=k: e.tensor_copy(self.bapc(XH + k * 512, C), tmp))
                w.drain("vector")
                w.op("vector", lambda e, tmp=tmp, k=k: e.tensor_sub(self.bapc(XL + k * 512, C), tmp, self.bapc(XH + k * 512, C)), inc=xnd)
            w.drain("vector")
        else:
            for k in range(KD):
                w.op("vector", lambda e, k=k: e.tensor_sub(self.apc(self.F_XN + k * 512, C), src(k), self.pb(4, C)))
            w.drain("vector")
            for k in range(KD):
                w.op("vector", lambda e, k=k: e.tensor_tensor(self.apc(self.F_XN + k * 512, C),
                                                              self.apc(self.F_XN + k * 512, C), self.pb(5, C), ALU.mult),
                     inc=xnd)
            w.drain("vector")

        wcount = 16 if mode == "bf16" else 32
        for g in range(NG):
            grp = FFN_GROUPS[g]
            for j in range(grp):
                hc = gbase[g] + j
                par = hc % 2
                gb, ub = par, 2 + par
                w.wait("tensor", sms["w%d" % par], b["w%d" % par] + wcount * (hc // 2 + 1))
                if hc >= 2:
                    w.wait("tensor", sil, b["si"] + hc - 1)
                    w.wait("tensor", gtd, b["gt"] + hc - 1)
                if mode == "bf16":
                    if hc == 0:
                        w.wait("tensor", xnd, b["xn"] + KD)
                    wb = WGUB + par * 8192
                    for k in range(KD):
                        xh_k = self.bapc(XH + k * 512, C)
                        xl_k = self.bapc(XL + k * 512, C)
                        w.mm(self.pb(gb, C), self.bapc(wb + k * 128, 128), xh_k, k == 0, False)
                        w.mm(self.pb(gb, C), self.bapc(wb + k * 128, 128), xl_k, False, False)
                        w.mm(self.pb(gb, C), self.bapc(wb + 4096 + k * 128, 128), xh_k, False, k == KD - 1,
                             inc=(mmg if k == KD - 1 else None))
                    for k in range(KD):
                        xh_k = self.bapc(XH + k * 512, C)
                        xl_k = self.bapc(XL + k * 512, C)
                        w.mm(self.pb(ub, C), self.bapc(wb + 2048 + k * 128, 128), xh_k, k == 0, False)
                        w.mm(self.pb(ub, C), self.bapc(wb + 2048 + k * 128, 128), xl_k, False, False)
                        w.mm(self.pb(ub, C), self.bapc(wb + 6144 + k * 128, 128), xh_k, False, k == KD - 1,
                             inc=(mmu if k == KD - 1 else None))
                else:
                    for k in range(KD):
                        if hc == 0:
                            w.wait("tensor", xnd, b["xn"] + k + 1)
                        w.mm(self.pb(gb, C), self.apc(self.F_WGU + par * 2 * KD * 128 + k * 128, 128),
                             self.apc(self.F_XN + k * 512, C), k == 0, k == KD - 1,
                             inc=(mmg if k == KD - 1 else None))
                    for k in range(KD):
                        w.mm(self.pb(ub, C), self.apc(self.F_WGU + par * 2 * KD * 128 + KD * 128 + k * 128, 128),
                             self.apc(self.F_XN + k * 512, C), k == 0, k == KD - 1,
                             inc=(mmu if k == KD - 1 else None))
                if mode == "split":
                    w.wait("scalar", sms["w%d" % par], b["w%d" % par] + wcount * (hc // 2 + 1))
                    w.wait("scalar", so, max(0, b["so"] + 32 * hc))
                    w.op("scalar", lambda e, par=par: e.activation(
                        out=self.bapc(SPH, 4096), in_=self.apc(self.F_WGU + par * 2 * KD * 128, 4096),
                        func=AF.Copy), inc=ch)
                    w.wait("vector", ch, b["ch"] + hc + 1)
                    w.op("vector", lambda e, par=par: e.tensor_sub(
                        self.bapc(SPL, 4096), self.apc(self.F_WGU + par * 2 * KD * 128, 4096),
                        self.bapc(SPH, 4096)), inc=cl)
                    w.drain("vector")
                w.wait("scalar", mmg, b["mg"] + hc + 1)
                if hc >= 2:
                    w.wait("scalar", gtd, b["gt"] + hc - 1)
                ssl = self.apc(self.SQT + (hc % 2) * 512, C)
                w.op("scalar", lambda e, ssl=ssl, gb=gb, C=C: e.activation(out=ssl, in_=self.pb(gb, C), func=AF.Silu), inc=sil)
                w.wait("vector", mmu, b["mu"] + hc + 1)
                w.wait("vector", sil, b["si"] + hc + 1)
                if mode == "bf16" and BF16_DOWN:
                    gt = self.apc(self.GTMP, C)
                    w.op("vector", lambda e, gt=gt, ssl=ssl, ub=ub, C=C: e.tensor_tensor(gt, ssl, self.pb(ub, C), ALU.mult))
                    w.drain("vector")
                    w.op("vector", lambda e, j=j, gt=gt, C=C: e.tensor_copy(
                        self.bapc(2 * self.F_GATED + j * 1024, C), gt))
                    w.drain("vector")
                    w.op("vector", lambda e, j=j, gt=gt, C=C: e.tensor_sub(
                        self.bapc(2 * self.F_GATED + j * 1024 + 512, C), gt,
                        self.bapc(2 * self.F_GATED + j * 1024, C)), inc=gtd)
                    w.drain("vector")
                else:
                    gsl = self.apc(self.F_GATED + j * 512, C)
                    w.op("vector", lambda e, gsl=gsl, ssl=ssl, ub=ub, C=C: e.tensor_tensor(gsl, ssl, self.pb(ub, C), ALU.mult), inc=gtd)
                    if j == grp - 1:
                        w.drain("vector")
            if mode == "split" and BF16_DOWN:
                SP2 = 2 * self.RING2 + 4 * KD * 128
                for j in range(grp):
                    hc = gbase[g] + j
                    w.wait("scalar", wdl, b["wd"] + 16 * (g + 1))
                    w.wait("scalar", s2, max(0, b["s2"] + 16 * hc))
                    w.op("scalar", lambda e, j=j: e.activation(
                        out=self.bapc(SP2, D), in_=self.apc(self.F_WD + j * D, D),
                        func=AF.Copy), inc=h2)
                    w.wait("vector", h2, b["h2"] + hc + 1)
                    w.op("vector", lambda e, j=j: e.tensor_sub(
                        self.bapc(SP2 + D, D), self.apc(self.F_WD + j * D, D),
                        self.bapc(SP2, D)), inc=c2)
                    w.drain("vector")
            wd_done = 16 * (gbase[g] + grp) if (mode == "bf16" and BF16_DOWN) else 16 * (g + 1)
            w.wait("tensor", wdl, b["wd"] + wd_done)
            w.wait("tensor", gtd, b["gt"] + gbase[g] + grp)
            for dch in range(KD):
                dpar = dch % 4
                idx = g * KD + dch
                if idx >= 4:
                    w.wait("tensor", acc, b["ac"] + idx - 3)
                if mode == "bf16" and BF16_DOWN:
                    for j in range(grp):
                        wh = self.bapc(2 * self.F_WD + j * 2 * D + dch * 128, 128)
                        wl = self.bapc(2 * self.F_WD + j * 2 * D + D + dch * 128, 128)
                        gh = self.bapc(2 * self.F_GATED + j * 1024, C)
                        gl = self.bapc(2 * self.F_GATED + j * 1024 + 512, C)
                        w.mm(self.pb(4 + dpar, C), wh, gh, j == 0, False)
                        w.mm(self.pb(4 + dpar, C), wh, gl, False, False)
                        w.mm(self.pb(4 + dpar, C), wl, gh, False, j == grp - 1,
                             inc=(dwn if j == grp - 1 else None))
                else:
                    for j in range(grp):
                        w.mm(self.pb(4 + dpar, C), self.apc(self.F_WD + j * D + dch * 128, 128),
                             self.apc(self.F_GATED + j * 512, C), j == 0, j == grp - 1,
                             inc=(dwn if j == grp - 1 else None))
                w.wait("vector", dwn, b["dn"] + idx + 1)
                w.op("vector", lambda e, dch=dch, dpar=dpar, C=C: e.scalar_tensor_tensor(
                    src(dch), self.pb(4 + dpar, C), half, src(dch), ALU.mult, ALU.add), inc=acc)
                if dch == KD - 1:
                    w.drain("vector")

    def emit_ffn_phase(self, w, tiles, dmas, modes=None):
        xld = self.sem("Axl")
        bx = xld.n
        for dst, srcap in dmas:
            w.dma(dst, srcap, inc=xld)
        for e in ("tensor", "scalar", "vector"):
            w.wait(e, xld, bx + 16 * len(dmas))
        for i, (src, C) in enumerate(tiles):
            m = modes[i] if modes else "f32"
            self.emit_ffn_tile(w, src, C, mode=m)

    def emit_qkv(self, w, st):
        ST, KV = self.ST, self.KV
        tag = "B"
        names = ("l0", "l1", "cd", "cr", "pj", "rv", "vr", "td", "tv", "rb", "cl", "rp")
        sms = {s: self.sem(tag + s) for s in names}
        b = self.snap(tag, names)
        csd, csr, pjd, rvd, vrd, trd, tvd, rbd, csl, rpd = (sms[s] for s in names[2:])
        nvb = 0

        if st == 0:
            ranges = [(lambda k, C: self.ring1(k, 0, C), WIN, 0, 0, False, False),
                      (lambda k, C: self.ring1(k, WIN, C), self.ST, WIN, WIN, True, False),
                      (lambda k, C: self.ring2(k, self.ST, C), SINK, self.EXT1 + self.ST, None, False, True)]
        else:
            ranges = [(lambda k, C: self.ring1(k, WIN, C), self.ST, WIN, 0, False, False),
                      (lambda k, C: self.ring2(k, 0, C), self.ST, self.EXT1, WIN, True, False)]

        n_ec = 0
        for ri, (srcf, C, toff, kvc, isq, sink) in enumerate(ranges):
            src = lambda k, srcf=srcf, C=C: srcf(k, C)
            w.wait("vector", pjd)
            bcs, mrow, negm, rrow = self.ln_stats(w, tag + "l", src, C, bcast_m=False)
            w.wait("vector", bcs)
            w.op("vector", lambda e, C=C: e.tensor_copy(self.apc(self.RSB, C), self.pb(5, C)), inc=rbd)
            w.drain("vector")
            w.wait("sync", rpd)
            w.dma(self.apc(self.CS, C), self.cos2[:, toff:toff + C], inc=csl)
            w.dma(self.apc(self.CS + 512, C), self.sin2[:, toff:toff + C], inc=csl)

            projs = [("k", self.wk, NGRP, NGRP * DH, 0, True),
                     ("v", self.wv, NGRP, NGRP * DH, 512, False)]
            if isq:
                projs.append(("q", self.wq, KD, D, 1024, True))
            for pname, wsrc, nec, wcols, sumoff, rope in projs:
                for ec in range(nec):
                    par = n_ec % 2
                    wbase = self.P_WS + par * KD * 128
                    w.wait("sync", pjd, max(0, b["pj"] + n_ec - 1))
                    w.dma(self.ap3(wbase, 128, KD, 1, 128),
                          bass.AP(wsrc.tensor, ec * 128, [[wcols, 128], [128 * wcols, KD], [1, 128]]), inc=sms["l%d" % par])
                    w.wait("tensor", sms["l%d" % par], b["l%d" % par] + 16 * (n_ec // 2 + 1))
                    sumrow = self.apc(self.P_SUM + sumoff + (0 if sumoff == 1024 else ec * 128), 128, parts=1)
                    need_sums = (ri == 0) or pname == "q"
                    if need_sums:
                        w.wait("tensor", csr)
                        for q4 in range(4):
                            w.mm(self.pb(4, 512, parts=1), self.apc(self.ONESC, 1),
                                 self.apc(wbase + q4 * 512, 512), q4 == 0, q4 == 3,
                                 inc=(csd if q4 == 3 else None))
                        w.wait("vector", csd)
                        w.op("vector", lambda e, sumrow=sumrow: e.tensor_reduce(
                            sumrow, bass.AP(self.ps, 4 * 512, [[4096, 1], [1, 128], [128, 4]]),
                            AX.X, ALU.add), inc=csr)
                        w.drain("vector")
                        w.wait("tensor", csr)
                    if n_ec >= 2:
                        w.wait("tensor", rvd, b["rv"] + n_ec - 1)
                    for k in range(KD):
                        w.mm(self.pb(par, C), self.apc(wbase + k * 128, 128), src(k), k == 0, False)
                    w.wait("tensor", rbd)
                    w.mm(self.pb(par, C), sumrow, negm, False, True, inc=pjd)
                    w.wait("vector", pjd, b["pj"] + n_ec + 1)
                    w.wait("vector", csl)
                    bcol = {"k": self.apc(self.BIASC + 2 * KD + ec, 1),
                            "v": self.apc(self.BIASC + 2 * KD + NGRP + ec, 1),
                            "q": self.apc(self.BIASC + ec, 1)}[pname]
                    rt0 = self.apc(self.RT, C)
                    rt1 = self.apc(self.RT + 512, C)
                    rt2 = self.apc(self.RT + 1024, C)
                    w.wait("vector", trd)
                    w.op("vector", lambda e, par=par, C=C: e.tensor_tensor(rt0, self.pb(par, C), self.apc(self.RSB, C), ALU.mult), inc=rvd)
                    w.drain("vector")
                    if pname == "q":
                        w.op("vector", lambda e, bcol=bcol, rt0=rt0: e.tensor_scalar(rt0, rt0, bcol, SCALE, ALU.add, ALU.mult), inc=vrd)
                    else:
                        w.op("vector", lambda e, bcol=bcol, rt0=rt0: e.tensor_scalar_add(rt0, rt0, bcol), inc=vrd)
                    w.drain("vector")
                    if pname == "q":
                        dst = self.apc(self.P_Q + ec * ST, C)
                    elif pname == "k":
                        dst = self.apc(self.KSINK + ec * SINK, SINK) if sink else self.apc(self.P_K + ec * KV + kvc, C)
                    else:
                        dst = None
                    if rope:
                        w.op("vector", lambda e, rt1=rt1, rt0=rt0: e.stream_shuffle(rt1, rt0, SWAP_ADJ))
                        w.drain("vector")
                        w.op("vector", lambda e, rt2=rt2, rt0=rt0, C=C: e.tensor_tensor(rt2, rt0, self.apc(self.CS, C), ALU.mult))
                        w.op("vector", lambda e, rt1=rt1, C=C: e.tensor_tensor(rt1, rt1, self.apc(self.CS + 512, C), ALU.mult))
                        w.drain("vector")
                        w.op("vector", lambda e, dst=dst, rt2=rt2, rt1=rt1: e.tensor_add(dst, rt2, rt1), inc=rpd)
                        w.drain("vector")
                    else:
                        nblk = (C + 127) // 128
                        for jj in range(nblk):
                            cc = min(128, C - jj * 128)
                            tb = 2 + nvb % 2
                            w.wait("tensor", vrd)
                            if nvb >= 2:
                                w.wait("tensor", tvd, b["tv"] + nvb - 1)
                            w.tr(self.pb(tb, 128, parts=cc),
                                 self.apc(self.RT + jj * 128, cc), self.apc(self.IDENT, 128), inc=trd)
                            if sink:
                                vdst = self.apc(self.VSINK + ec * 128, 128, parts=cc)
                            else:
                                vdst = self.apc(self.P_V + (kvc // 128 + jj) * 512 + ec * 128, 128, parts=cc)
                            w.wait("scalar", trd)
                            w.op("scalar", lambda e, vdst=vdst, tb=tb, cc=cc: e.activation(
                                out=vdst, in_=self.pb(tb, 128, parts=cc), func=AF.Copy), inc=tvd)
                            nvb += 1
                    n_ec += 1

    def emit_attn(self, w, st):
        ST, KV, NQT, MCOL = self.ST, self.KV, self.NQT, self.MCOL
        tag = "C"
        names = ("ml", "qk", "mk", "ex", "sm", "pn", "pt", "pe", "pv", "av", "o0", "o1", "oo", "xb")
        sms = {s: self.sem(tag + s) for s in names}
        b = self.snap(tag, names)
        mld, qkd, mkd, exd, smd, pnd, ptd, pte, pvd, ave = (sms[s] for s in names[:10])
        ood, xbd = sms["oo"], sms["xb"]

        nmx = self.apc(self.NMX, 1)
        ssm = self.apc(self.NMX + 1, 1)
        rcp = self.apc(self.NMX + 2, 1)
        NIT = NQT * NHEAD

        def emit_qk(it):
            i, h = divmod(it, NHEAD)
            g = h // GSZ
            p = it % 2
            if i > 0 and h == 0:
                w.wait("sync", mkd, b["mk"] + i * NHEAD)
                w.dma(self.apc(self.P_MASK, MCOL),
                      bass.AP(self.maskd.tensor, (st * NQT + i) * 128 * MCOL, [[MCOL, 128], [1, MCOL]]),
                      inc=mld)
            if it >= 2:
                w.wait("tensor", exd, b["ex"] + it - 1)
            qsl = self.apc(self.P_Q + h * ST + i * 128, 128)
            w.mm(self.pb(2 * p, 512), qsl, self.apc(self.P_K + g * KV + i * 128, 512), True, True)
            w.mm(self.pb(2 * p + 1, 128), qsl, self.apc(self.P_K + g * KV + i * 128 + 512, 128), True, False)
            w.mm(self.pb(2 * p + 1, SINK, c0=128), qsl, self.apc(self.KSINK + g * SINK, SINK), False, True, inc=qkd)

        def emit_softmax(it):
            p = it % 2
            w.wait("vector", qkd, b["qk"] + it + 1)
            w.wait("vector", mld)
            if it >= 1:
                w.wait("vector", pnd, b["pn"] + it)
            w.op("vector", lambda e, p=p: e.tensor_tensor(self.psp(p * 1024, 644), self.psp(p * 1024, 644),
                                                          self.apc(self.P_MASK, 644), ALU.add))
            w.drain("vector")
            w.op("vector", lambda e, p=p: e.tensor_reduce(nmx, self.psp(p * 1024, 644), AX.X, ALU.max, negate=True), inc=mkd)
            w.wait("scalar", mkd, b["mk"] + it + 1)
            w.op("scalar", lambda e, p=p: e.activation(out=self.apc(self.P_P, 644), in_=self.psp(p * 1024, 644),
                                                       func=AF.Exp, bias=nmx, scale=1.0, accum_out=ssm), inc=exd)
            w.wait("vector", exd, b["ex"] + it + 1)
            w.op("vector", lambda e: e.reciprocal(rcp, ssm), inc=smd)
            w.wait("scalar", smd, b["sm"] + it + 1)
            w.wait("scalar", ptd, b["pt"] + it)
            w.op("scalar", lambda e: e.activation(out=self.apc(self.P_PN, 644), in_=self.apc(self.P_P, 644),
                                                  func=AF.Copy, scale=rcp), inc=pnd)

        def emit_pv(it):
            i, h = divmod(it, NHEAD)
            g = h // GSZ
            w.wait("tensor", pnd, b["pn"] + it + 1)
            if it >= 1:
                w.wait("tensor", pte, b["pe"] + it)
            for j in range(4):
                w.tr(self.pb(4, 128, c0=j * 128), self.apc(self.P_PN + j * 128, 128), self.apc(self.IDENT, 128))
            w.tr(self.pb(5, 128), self.apc(self.P_PN + 512, 128), self.apc(self.IDENT, 128))
            w.tr(self.pb(5, 128, c0=128, parts=SINK), self.apc(self.P_PN + 640, SINK), self.apc(self.IDENT, 128), inc=ptd)
            w.wait("scalar", ptd, b["pt"] + it + 1)
            w.op("scalar", lambda e: e.activation(out=self.apc(self.P_PT, 512), in_=self.pb(4, 512), func=AF.Copy))
            w.op("scalar", lambda e: e.activation(out=self.apc(self.P_PT + 512, 256), in_=self.pb(5, 256), func=AF.Copy), inc=pte)
            w.drain("scalar")
            par = 6 + it % 2
            w.wait("tensor", pte, b["pe"] + it + 1)
            if it >= 2:
                w.wait("tensor", ave, b["av"] + it - 1)
            for j in range(5):
                w.mm(self.pb(par, 128), self.apc(self.P_V + (i + j) * 512 + g * 128, 128),
                     self.apc(self.P_PT + j * 128, 128), j == 0, False)
            w.mm(self.pb(par, 128), self.apc(self.VSINK + g * 128, 128, parts=SINK),
                 self.apc(self.P_PT + 640, 128, parts=SINK), False, True, inc=pvd)
            w.wait("scalar", pvd, b["pv"] + it + 1)
            w.op("scalar", lambda e, h=h, i=i, par=par: e.activation(
                out=self.apc(self.P_Q + h * ST + i * 128, 128), in_=self.pb(par, 128), func=AF.Copy), inc=ave)

        w.dma(self.apc(self.P_MASK, MCOL),
              bass.AP(self.maskd.tensor, (st * NQT) * 128 * MCOL, [[MCOL, 128], [1, MCOL]]),
              inc=mld)
        for it in range(NIT + 1):
            if it < NIT:
                emit_qk(it)
                emit_softmax(it)
            if it >= 1:
                emit_pv(it - 1)

        res = (lambda k, c0, c: self.ring1(k, WIN + c0, c)) if st == 0 else (lambda k, c0, c: self.ring2(k, c0, c))
        xb = (lambda k, c0, c: self.ring1(k, c0, c)) if st == 0 else (lambda k, c0, c: self.ring1(k, WIN + c0, c))
        for ec in range(KD):
            par = 6 + ec % 2
            wbase = self.P_WS + (ec % 2) * KD * 128
            w.wait("sync", ood, max(0, b["oo"] + ec - 1))
            w.dma(self.ap3(wbase, 128, KD, 1, 128),
                  bass.AP(self.wo.tensor, ec * 128, [[D, 128], [128 * D, KD], [1, 128]]), inc=sms["o%d" % (ec % 2)])
            w.wait("tensor", sms["o%d" % (ec % 2)], b["o%d" % (ec % 2)] + 16 * (ec // 2 + 1))
            if ec == 0:
                w.wait("tensor", ave, b["av"] + NQT * NHEAD)
            if ec >= 2:
                w.wait("tensor", xbd, b["xb"] + ec - 1)
            for cc in range(KD):
                w.mm(self.pb(par, ST), self.apc(wbase + cc * 128, 128),
                     self.apc(self.P_Q + cc * ST, ST),
                     cc == 0, cc == KD - 1, inc=(ood if cc == KD - 1 else None))
            w.wait("vector", ood, b["oo"] + ec + 1)
            for ct in range(NQT):
                w.op("vector", lambda e, ec=ec, ct=ct, par=par: e.tensor_add(
                    xb(ec, ct * 128, 128), res(ec, ct * 128, 128), self.pb(par, 128, c0=ct * 128)))
            w.drain("vector")
            w.op("vector", lambda e, ec=ec, xb=xb, ST=ST: e.tensor_scalar_add(
                xb(ec, 0, ST), xb(ec, 0, ST), self.apc(self.BIASC + KD + ec, 1)), inc=xbd)
            w.drain("vector")

    def emit_out(self, w):
        so = self.sem("out")
        bo = so.n
        w.dma(bass.AP(self.yt.tensor, 0, [[self.OWN, 128], [128 * self.OWN, KD], [1, self.ST]]),
              self.ap3(self.RING1, self.EXT1, KD, 1, self.ST), inc=so)
        w.dma(bass.AP(self.yt.tensor, self.ST, [[self.OWN, 128], [128 * self.OWN, KD], [1, self.ST]]),
              self.ap3(self.RING1 + WIN, self.EXT1, KD, 1, self.ST), inc=so)
        w.wait("sync", so, bo + 32)

    def build(self):
        ST = self.ST
        self.phase("pre", self.emit_preload)
        a1_dma = [(self.ap3(self.RING1, self.EXT1, KD, 1, self.EXT1),
                   bass.AP(self.xe.tensor, 0, [[self.EXT1, 128], [128 * self.EXT1, KD], [1, self.EXT1]]))]
        a1_tiles = [((lambda k, c0=c0, C=C: self.ring1(k, c0, C)), C) for c0, C in balanced_split(self.EXT1)]
        a1_modes = (["split"] + ["bf16"] * (len(a1_tiles) - 1)) if BF16_GU else None
        self.phase("A1", lambda w: self.emit_ffn_phase(w, a1_tiles, a1_dma, a1_modes))
        a2_dma = [(self.ap3(self.RING2, self.R2, KD, 1, ST),
                   bass.AP(self.xo2.tensor, 0, [[ST, 128], [128 * ST, KD], [1, ST]])),
                  (self.ap3(self.RING2 + ST, self.R2, KD, 1, SINK),
                   bass.AP(self.xsink.tensor, 0, [[SINK, 128], [128 * SINK, KD], [1, SINK]]))]
        a2_tiles = [((lambda k, c0=c0, C=C: self.ring2(k, c0, C)), C) for c0, C in balanced_split(self.R2)]
        bfm = lambda ts: (["bf16"] * len(ts)) if BF16_GU else None
        self.phase("A2", lambda w: self.emit_ffn_phase(w, a2_tiles, a2_dma, bfm(a2_tiles)))
        self.phase("B1", lambda w: self.emit_qkv(w, 0))
        self.phase("C1", lambda w: self.emit_attn(w, 0))
        d1_tiles = [((lambda k, c0=c0, C=C: self.ring1(k, c0, C)), C) for c0, C in balanced_split(ST)]
        self.phase("D1", lambda w: self.emit_ffn_phase(w, d1_tiles, [], bfm(d1_tiles)))
        self.phase("B2", lambda w: self.emit_qkv(w, 1))
        self.phase("C2", lambda w: self.emit_attn(w, 1))
        d2_tiles = [((lambda k, c0=c0, C=C: self.ring1(k, WIN + c0, C)), C) for c0, C in balanced_split(ST)]
        self.phase("D2", lambda w: self.emit_ffn_phase(w, d2_tiles, [], bfm(d2_tiles)))
        self.phase("out", self.emit_out)
        return self.nc



def rope_tables(positions):
    half = 64
    inv = 1.0 / (10000.0 ** (np.arange(half, dtype=np.float32) * 2.0 / DH))
    ang = positions[None, :].astype(np.float32) * inv[:, None]
    c, s = np.cos(ang), np.sin(ang)
    cos2 = np.empty((128, ang.shape[1]), np.float32)
    sin2 = np.empty((128, ang.shape[1]), np.float32)
    cos2[0::2] = c
    cos2[1::2] = c
    sin2[0::2] = -s
    sin2[1::2] = s
    return cos2, sin2


def perm_cols():
    p = np.empty(DH, np.int64)
    p[0::2] = np.arange(64)
    p[1::2] = np.arange(64) + 64
    return p


def build_masks(own, p0):
    STl = own // 2
    NQT = STl // 128
    MCOL = WIN + 128 + SINK
    out = np.zeros((2 * NQT, 128, MCOL), np.float32)
    for st in range(2):
        for i in range(NQT):
            q0g = p0 + st * STl + i * 128
            rows = q0g + np.arange(128)[:, None]
            kw = (q0g - WIN) + np.arange(WIN + 128)[None, :]
            valid = (kw >= 0) & (kw <= rows) & (kw >= rows - WIN)
            m = out[st * NQT + i]
            m[:, :WIN + 128] = np.where(valid, 0.0, NEG)
            ks = np.arange(SINK)[None, :]
            vs = (ks <= rows) & (rows - ks > WIN)
            m[:, WIN + 128:] = np.where(vs, 0.0, NEG)
    return out


_BUILD_CACHE = {}


def get_nc(own):
    if own not in _BUILD_CACHE:
        _BUILD_CACHE[own] = Builder(own).build()
    return _BUILD_CACHE[own]


def make_core_inputs(xt, weights, own, b, p0, ttot):
    T = xt.shape[2]
    ext0 = p0 - WIN
    xe = np.zeros((D, WIN + own // 2), np.float32)
    lo = max(0, ext0)
    xe[:, lo - ext0:] = xt[b, :, lo:p0 + own // 2]
    d = dict(weights)
    d["xe"] = xe
    d["xo2"] = np.ascontiguousarray(xt[b, :, p0 + own // 2:p0 + own])
    d["xsink"] = np.ascontiguousarray(xt[b, :, 0:SINK])
    pos = np.concatenate([np.maximum(np.arange(ext0, p0 + own // 2), 0),
                          np.arange(p0 + own // 2, p0 + own),
                          np.arange(SINK)]).astype(np.float32)
    assert pos.shape[0] == ttot
    c2, s2 = rope_tables(pos)
    d["cos2"] = c2
    d["sin2"] = s2
    d["maskd"] = build_masks(own, p0).reshape(-1, WIN + 128 + SINK)
    return d


def prep_weights(L0_attn_Wq, L0_attn_bq, L0_attn_Wk, L0_attn_bk, L0_attn_Wv,
                 L0_attn_bv, L0_attn_Wo, L0_attn_bo, L0_ffn_w_gate,
                 L0_ffn_w_up, L0_ffn_w_down):
    pc = perm_cols()
    qperm = np.concatenate([h * DH + pc for h in range(NHEAD)])
    kperm = np.concatenate([g * DH + pc for g in range(NGRP)])
    w = {}
    w["wgt"] = np.ascontiguousarray(np.asarray(L0_ffn_w_gate).T)
    w["wut"] = np.ascontiguousarray(np.asarray(L0_ffn_w_up).T)
    w["wdt"] = np.ascontiguousarray(np.asarray(L0_ffn_w_down).T)
    w["wq"] = np.ascontiguousarray(np.asarray(L0_attn_Wq)[:, qperm])
    w["wk"] = np.ascontiguousarray(np.asarray(L0_attn_Wk)[:, kperm])
    w["wv"] = np.ascontiguousarray(np.asarray(L0_attn_Wv))
    w["wo"] = np.ascontiguousarray(np.asarray(L0_attn_Wo))
    w["bqc"] = np.ascontiguousarray(np.asarray(L0_attn_bq)[qperm].reshape(KD, 128).T)
    w["bkc"] = np.ascontiguousarray(np.asarray(L0_attn_bk)[kperm].reshape(NGRP, 128).T)
    w["bvc"] = np.ascontiguousarray(np.asarray(L0_attn_bv).reshape(NGRP, 128).T)
    w["boc"] = np.ascontiguousarray(np.asarray(L0_attn_bo).reshape(KD, 128).T)
    return w


def kernel(L0_attn_x, L0_attn_Wq, L0_attn_bq, L0_attn_Wk, L0_attn_bk,
           L0_attn_Wv, L0_attn_bv, L0_attn_Wo, L0_attn_bo, L0_ffn_w_gate,
           L0_ffn_w_up, L0_ffn_w_down):
    x = np.asarray(L0_attn_x, dtype=np.float32)
    B, T, _ = x.shape
    n_cores = 8
    chunks = n_cores // B
    own = T // chunks
    xt = np.ascontiguousarray(x.transpose(0, 2, 1))
    weights = prep_weights(L0_attn_Wq, L0_attn_bq, L0_attn_Wk, L0_attn_bk,
                           L0_attn_Wv, L0_attn_bv, L0_attn_Wo, L0_attn_bo,
                           L0_ffn_w_gate, L0_ffn_w_up, L0_ffn_w_down)
    ttot = WIN + own + SINK
    in_maps = []
    for c in range(n_cores):
        b, j = c // chunks, c % chunks
        in_maps.append(make_core_inputs(xt, weights, own, b, j * own, ttot))
    nc = get_nc(own)
    res = run_bass_kernel_spmd(nc, in_maps, core_ids=list(range(n_cores)))
    yt = np.empty((B, D, T), np.float32)
    for c in range(n_cores):
        b, j = c // chunks, c % chunks
        yt[b, :, j * own:(j + 1) * own] = res.results[c]["yt"]
    return np.ascontiguousarray(yt.transpose(0, 2, 1))


# version 16
# speedup vs baseline: 1.0665x; 1.0372x over previous
import numpy as np
import concourse.bass as bass
import concourse.mybir as mybir
from concourse.bass_utils import run_bass_kernel_spmd

F32 = mybir.dt.float32
AF = mybir.ActivationFunctionType
ALU = mybir.AluOpType
AX = mybir.AxisListType

D = 2048
HID = 8192
KD = D // 128
KH = HID // 128
NHEAD = 16
NGRP = 4
GSZ = NHEAD // NGRP
DH = 128
WIN = 512
SINK = 4
EPS = 1e-5
SCALE = float(DH) ** -0.5
NEG = -1.0e30
FFN_GROUPS = [3] * 21 + [1]
GRP_MAX = 3
BF16_GU = True
BF16_DOWN = True

ENGS = ("sync", "tensor", "scalar", "vector")
SWAP_ADJ = [i ^ 1 for i in range(32)]


class Sem:
    __slots__ = ("h", "n")

    def __init__(self, nc, name):
        self.h = nc.alloc_semaphore(name=name)
        self.n = 0


class W:

    def __init__(self, eng_name, eng):
        self.e = eng_name
        self.eng = eng

    def mm(self, out, lhsT, rhs, start, stop, inc=None):
        if self.e == "tensor":
            ins = self.eng.matmul(out, lhsT, rhs, start=start, stop=stop)
            if inc is not None:
                ins.then_inc(inc.h, 1)
        if inc is not None:
            inc.n += 1

    def tr(self, out, in_, ident, inc=None):
        if self.e == "tensor":
            ins = self.eng.transpose(out, in_, ident)
            if inc is not None:
                ins.then_inc(inc.h, 1)
        if inc is not None:
            inc.n += 1

    def op(self, engine, fn, inc=None, inck=1):
        if self.e == engine:
            ins = fn(self.eng)
            if inc is not None:
                ins.then_inc(inc.h, inck)
        if inc is not None:
            inc.n += inck

    def dma(self, out, in_, inc=None):
        if self.e == "sync":
            ins = self.eng.dma_start(out=out, in_=in_)
            if inc is not None:
                ins.then_inc(inc.h, 16)
        if inc is not None:
            inc.n += 16

    def wait(self, engine, sem, val=None):
        if self.e == engine:
            self.eng.wait_ge(sem.h, sem.n if val is None else val)

    def drain(self, engine):
        if self.e == engine:
            self.eng.drain()


def balanced_split(total, cap=512):
    n = -(-total // cap)
    base = -(-total // n)
    out, o = [], 0
    while o < total:
        c = min(base, total - o)
        out.append((o, c))
        o += c
    return out


class Builder:
    def __init__(self, own):
        self.OWN = own
        self.ST = own // 2
        self.EXT1 = WIN + self.ST
        self.R2 = self.ST + SINK
        self.NQT = self.ST // 128
        self.KV = WIN + self.ST
        self.TTOT = self.EXT1 + self.ST + SINK
        self.MCOL = WIN + 128 + SINK
        nc = bass.Bass("TRN2", target_bir_lowering=False)
        self.nc = nc

        def t(n, s):
            return nc.dram_tensor(n, s, F32, kind="ExternalInput").ap()

        self.xe = t("xe", [D, self.EXT1])
        self.xo2 = t("xo2", [D, self.ST])
        self.xsink = t("xsink", [D, SINK])
        self.wgt = t("wgt", [D, HID])
        self.wut = t("wut", [D, HID])
        self.wdt = t("wdt", [HID, D])
        self.wq = t("wq", [D, D])
        self.wk = t("wk", [D, NGRP * DH])
        self.wv = t("wv", [D, NGRP * DH])
        self.wo = t("wo", [D, D])
        self.bqc = t("bqc", [128, KD])
        self.bkc = t("bkc", [128, NGRP])
        self.bvc = t("bvc", [128, NGRP])
        self.boc = t("boc", [128, KD])
        self.cos2 = t("cos2", [128, self.TTOT])
        self.sin2 = t("sin2", [128, self.TTOT])
        self.maskd = t("maskd", [2 * self.NQT * 128, self.MCOL])
        self.yt = nc.dram_tensor("yt", [D, self.OWN], F32, kind="ExternalOutput").ap()

        self.ACOLS = 52600
        self.sb = nc.alloc_sbuf_tensor("arena", [128, self.ACOLS], F32)
        o = [0]

        def region(n):
            r = o[0]
            o[0] += n
            return r

        self.RING1 = region(KD * self.EXT1)
        self.RING2 = region(KD * self.R2)
        PH = region(24832)
        self.PHASE = PH
        self.IDENT = region(128)
        self.ONESC = region(1)
        self.ONESR = region(128)
        self.BIASC = region(2 * KD + 2 * NGRP)
        self.KSINK = region(NGRP * SINK)
        self.VSINK = region(NGRP * DH)
        self.SROW = region(2 * 528)
        self.SQT = region(1024)
        self.NMX = region(4)
        self.GT2 = self.SQT
        assert o[0] <= self.ACOLS, o[0]

        self.F_XN = PH
        self.F_GATED = PH + KD * 512
        self.F_WGU = self.F_GATED + GRP_MAX * 512
        self.F_WD = self.F_WGU + 2 * (2 * KD * 128)
        assert self.F_WD + GRP_MAX * D - PH <= 24832
        self.P_Q = PH
        self.P_K = self.P_Q + KD * self.ST
        self.P_V = self.P_K + NGRP * self.KV
        self.P_WS = self.P_V + (self.KV // 128) * 512
        self.P_SUM = self.P_WS + 2 * KD * 128
        SCR = self.P_SUM + 1152
        self.CS = SCR
        self.RSB = SCR + 1024
        self.RT = SCR + 1536
        self.P_MASK = SCR
        self.P_P = SCR + 648
        self.P_PN = SCR + 1296
        self.P_PT = SCR + 1944
        assert SCR + 3072 - PH <= 24832, SCR + 3072 - PH

        self.ps = nc.alloc_psum_tensor("psarena", [128, 4096], F32)
        self.bb = self.sb.bitcast(mybir.dt.bfloat16)
        self.sc_wgu = nc.dram_tensor("sc_wgu", [KH, 128, 4 * KD * 128],
                                     mybir.dt.bfloat16, kind="Internal")
        self.sc_wd = nc.dram_tensor("sc_wd", [KH, 128, 2 * D],
                                    mybir.dt.bfloat16, kind="Internal")
        self.GTMP = self.PHASE + 24064
        self.sems = {}
        self._consts = {}

    def apc(self, base, cols, part0=0, parts=128):
        return self.sb.ap()[part0:part0 + parts, base:base + cols]

    def ap3(self, base, d1, n1, d2, n2):
        return bass.AP(self.sb, base, [[self.ACOLS, 128], [d1, n1], [d2, n2]])

    def bapc(self, bcol, cols, part0=0, parts=128):
        return self.bb.ap()[part0:part0 + parts, bcol:bcol + cols]

    def bp3(self, bcol, d1, n1, d2, n2):
        return bass.AP(self.bb, bcol, [[2 * self.ACOLS, 128], [d1, n1], [d2, n2]])

    def pb(self, bank, cols=512, c0=0, parts=128):
        return self.ps.ap()[0:parts, bank * 512 + c0:bank * 512 + c0 + cols]

    def psp(self, c0, cols):
        return self.ps.ap()[:, c0:c0 + cols]

    def ring1(self, k, c0, c):
        return self.apc(self.RING1 + k * self.EXT1 + c0, c)

    def ring2(self, k, c0, c):
        return self.apc(self.RING2 + k * self.R2 + c0, c)

    def sem(self, name):
        if name not in self.sems:
            self.sems[name] = Sem(self.nc, name)
        return self.sems[name]

    def snap(self, tag, names):
        return {s: self.sem(tag + s).n for s in names}

    def const(self, name, arr):
        if name not in self._consts:
            self._consts[name] = self.nc.inline_tensor(np.ascontiguousarray(arr), name)
        return self._consts[name]

    def phase(self, name, emit):
        presnap = {k: s.n for k, s in self.sems.items()}
        counts = None
        with self.nc.Block(name) as blk:
            for e in ENGS:
                for k, s in self.sems.items():
                    s.n = presnap.get(k, 0)
                w = W(e, getattr(self.nc, e))
                getattr(blk, e)(lambda eng, w=w: emit(w))
                chk = {k: s.n for k, s in self.sems.items()}
                if counts is None:
                    counts = chk
                else:
                    assert chk == counts, f"walk divergence in {name}"

    def emit_preload(self, w):
        sl = self.sem("pre")
        w.dma(self.apc(self.IDENT, 128), self.const("identc", np.eye(128, dtype=np.float32)).ap(), inc=sl)
        w.dma(self.apc(self.ONESC, 1), self.const("onescc", np.ones((128, 1), np.float32)).ap(), inc=sl)
        w.dma(self.apc(self.ONESR, 128, parts=1), self.const("onesrc", np.ones((1, 128), np.float32)).ap(), inc=sl)
        w.dma(self.apc(self.BIASC, KD), self.bqc, inc=sl)
        w.dma(self.apc(self.BIASC + KD, KD), self.boc, inc=sl)
        w.dma(self.apc(self.BIASC + 2 * KD, NGRP), self.bkc, inc=sl)
        w.dma(self.apc(self.BIASC + 2 * KD + NGRP, NGRP), self.bvc, inc=sl)
        for e in ("tensor", "scalar", "vector"):
            w.wait(e, sl)

    def ln_stats(self, w, tag, src, C, bcast_m=True):
        sq = self.sem(tag + "sq")
        st = self.sem(tag + "st")
        vr = self.sem(tag + "vr")
        rs = self.sem(tag + "rs")
        bc = self.sem(tag + "bc")
        b = self.snap(tag, ("sq", "st", "vr", "rs", "bc"))
        mrow = self.apc(self.SROW, C, parts=1)
        negm = mrow if not bcast_m else None
        rrow = self.apc(self.SROW + 528, C, parts=1)
        t2 = self.apc(self.SQT, C, parts=1)
        w.wait("vector", bc, b["bc"])
        w.wait("tensor", vr, b["vr"])
        w.wait("tensor", self.sem("Aac"), self.sem("Aac").n)
        for k in range(KD):
            tsl = self.apc(self.SQT + (k % 2) * 512, C)
            if k >= 2:
                w.wait("scalar", st, b["st"] + k - 1)
            w.op("scalar", lambda e, tsl=tsl, k=k: e.activation(out=tsl, in_=src(k), func=AF.Square), inc=sq)
            w.wait("tensor", sq, b["sq"] + k + 1)
            w.mm(self.pb(6, C, parts=1), self.apc(self.ONESC, 1), src(k), k == 0, k == KD - 1)
            w.mm(self.pb(7, C, parts=1), self.apc(self.ONESC, 1), tsl, k == 0, k == KD - 1,
                 inc=(st if True else None))
        w.wait("vector", st, b["st"] + KD)
        w.op("vector", lambda e: e.tensor_scalar_mul(mrow, self.pb(6, C, parts=1), 1.0 / D))
        w.op("vector", lambda e: e.tensor_scalar_mul(rrow, self.pb(7, C, parts=1), 1.0 / D))
        w.drain("vector")
        w.op("vector", lambda e: e.tensor_tensor(t2, mrow, mrow, ALU.mult))
        w.drain("vector")
        if not bcast_m:
            w.op("vector", lambda e: e.tensor_scalar_mul(mrow, mrow, -1.0))
        w.op("vector", lambda e: e.tensor_sub(rrow, rrow, t2))
        w.drain("vector")
        w.op("vector", lambda e: e.tensor_scalar_add(rrow, rrow, EPS), inc=vr)
        w.wait("scalar", vr, b["vr"] + 1)
        w.op("scalar", lambda e: e.activation(out=t2, in_=rrow, func=AF.Ln))
        w.drain("scalar")
        w.op("scalar", lambda e: e.activation(out=rrow, in_=t2, func=AF.Exp, scale=-0.5), inc=rs)
        w.wait("tensor", rs, b["rs"] + 1)
        if bcast_m:
            w.mm(self.pb(4, C), self.apc(self.ONESR, 128, parts=1), mrow, True, True)
        w.mm(self.pb(5, C), self.apc(self.ONESR, 128, parts=1), rrow, True, True, inc=bc)
        return bc, mrow, negm, rrow

    def emit_ffn_tile(self, w, src, C, half=0.5, mode="f32"):
        tag = "A"
        names = ("xn", "w0", "w1", "wd", "mg", "mu", "si", "gt", "dn", "ac",
                 "ch", "cl", "so", "xc", "h2", "c2", "s2", "gh", "gl")
        sms = {s: self.sem(tag + s) for s in names}
        b = self.snap(tag, names)
        xnd, wdl = sms["xn"], sms["wd"]
        mmg, mmu, sil, gtd, dwn, acc = (sms[s] for s in ("mg", "mu", "si", "gt", "dn", "ac"))
        ch, cl, so, xc = (sms[s] for s in ("ch", "cl", "so", "xc"))
        h2, c2, s2, ghs, gls = (sms[s] for s in ("h2", "c2", "s2", "gh", "gl"))
        WGUB = 2 * self.F_WGU
        XH = 2 * self.F_XN
        XL = XH + KD * 512
        SPH = 2 * self.RING2
        SPL = SPH + 2 * KD * 128

        NG = len(FFN_GROUPS)
        gbase = [sum(FFN_GROUPS[:i]) for i in range(NG)]
        for g in range(NG):
            grp = FFN_GROUPS[g]
            for j in range(grp):
                hc = gbase[g] + j
                par = hc % 2
                w.wait("sync", mmu, max(0, b["mu"] + hc - 1))
                if mode == "bf16":
                    w.wait("sync", so, 32 * (hc + 1))
                    w.dma(self.bapc(WGUB + par * 8192, 8192),
                          bass.AP(self.sc_wgu, hc * 128 * 8192, [[8192, 128], [1, 8192]]),
                          inc=sms["w%d" % par])
                else:
                    if mode == "split":
                        w.wait("sync", cl, max(0, b["cl"] + hc - 1))
                    w.dma(self.ap3(self.F_WGU + par * 2 * KD * 128, 128, KD, 1, 128),
                          bass.AP(self.wgt.tensor, hc * 128, [[HID, 128], [128 * HID, KD], [1, 128]]), inc=sms["w%d" % par])
                    w.dma(self.ap3(self.F_WGU + par * 2 * KD * 128 + KD * 128, 128, KD, 1, 128),
                          bass.AP(self.wut.tensor, hc * 128, [[HID, 128], [128 * HID, KD], [1, 128]]), inc=sms["w%d" % par])
                if mode == "split":
                    if hc > 0:
                        w.wait("sync", cl, b["cl"] + hc)
                        w.dma(bass.AP(self.sc_wgu, (hc - 1) * 128 * 8192, [[8192, 128], [1, 4096]]),
                              self.bapc(SPH, 4096), inc=so)
                        w.dma(bass.AP(self.sc_wgu, (hc - 1) * 128 * 8192 + 4096, [[8192, 128], [1, 4096]]),
                              self.bapc(SPL, 4096), inc=so)
                    if hc == KH - 1:
                        w.wait("sync", cl, b["cl"] + hc + 1)
                        w.dma(bass.AP(self.sc_wgu, hc * 128 * 8192, [[8192, 128], [1, 4096]]),
                              self.bapc(SPH, 4096), inc=so)
                        w.dma(bass.AP(self.sc_wgu, hc * 128 * 8192 + 4096, [[8192, 128], [1, 4096]]),
                              self.bapc(SPL, 4096), inc=so)
            if mode == "bf16" and BF16_DOWN:
                w.wait("sync", dwn, max(0, b["dn"] + g * KD))
                for j in range(grp):
                    hc = gbase[g] + j
                    w.wait("sync", s2, 16 * (hc + 1))
                    w.dma(self.bapc(2 * self.F_WD + j * 2 * D, 2 * D),
                          bass.AP(self.sc_wd, hc * 128 * 2 * D, [[2 * D, 128], [1, 2 * D]]),
                          inc=wdl)
            else:
                w.wait("sync", dwn, max(0, b["dn"] + g * KD))
                if mode == "split" and BF16_DOWN:
                    w.wait("sync", c2, max(0, b["c2"] + gbase[g]))
                w.dma(self.ap3(self.F_WD, D, grp, 1, D),
                      bass.AP(self.wdt.tensor, gbase[g] * 128 * D, [[D, 128], [128 * D, grp], [1, D]]),
                      inc=wdl)
                if mode == "split" and BF16_DOWN:
                    for j in range(grp):
                        hc = gbase[g] + j
                        w.wait("sync", c2, b["c2"] + hc + 1)
                        w.dma(bass.AP(self.sc_wd, hc * 128 * 2 * D, [[2 * D, 128], [1, 2 * D]]),
                              self.bapc(2 * self.RING2 + 4 * KD * 128, 2 * D), inc=s2)

        w.wait("scalar", gtd, b["gt"])
        w.wait("vector", mmu, b["mu"])

        bc, mrow, negm, rrow = self.ln_stats(w, tag + "l", src, C, bcast_m=True)
        w.wait("vector", bc)
        if mode == "bf16":
            for k in range(KD):
                tmp = self.apc(self.GT2 + (k % 2) * 512, C)
                w.op("vector", lambda e, tmp=tmp, k=k: e.tensor_sub(tmp, src(k), self.pb(4, C)))
                w.drain("vector")
                w.op("vector", lambda e, tmp=tmp: e.tensor_tensor(tmp, tmp, self.pb(5, C), ALU.mult))
                w.drain("vector")
                w.op("vector", lambda e, tmp=tmp, k=k: e.tensor_copy(self.bapc(XH + k * 512, C), tmp))
                w.drain("vector")
                w.op("vector", lambda e, tmp=tmp, k=k: e.tensor_sub(self.bapc(XL + k * 512, C), tmp, self.bapc(XH + k * 512, C)), inc=xnd)
            w.drain("vector")
        else:
            for k in range(KD):
                w.op("vector", lambda e, k=k: e.tensor_sub(self.apc(self.F_XN + k * 512, C), src(k), self.pb(4, C)))
            w.drain("vector")
            for k in range(KD):
                w.op("vector", lambda e, k=k: e.tensor_tensor(self.apc(self.F_XN + k * 512, C),
                                                              self.apc(self.F_XN + k * 512, C), self.pb(5, C), ALU.mult),
                     inc=xnd)
            w.drain("vector")

        wcount = 16 if mode == "bf16" else 32
        for g in range(NG):
            grp = FFN_GROUPS[g]
            for j in range(grp):
                hc = gbase[g] + j
                par = hc % 2
                gb, ub = par, 2 + par
                w.wait("tensor", sms["w%d" % par], b["w%d" % par] + wcount * (hc // 2 + 1))
                if hc >= 2:
                    w.wait("tensor", sil, b["si"] + hc - 1)
                    w.wait("tensor", gtd, b["gt"] + hc - 1)
                if mode == "bf16":
                    if hc == 0:
                        w.wait("tensor", xnd, b["xn"] + KD)
                    wb = WGUB + par * 8192
                    for k in range(KD):
                        xh_k = self.bapc(XH + k * 512, C)
                        xl_k = self.bapc(XL + k * 512, C)
                        w.mm(self.pb(gb, C), self.bapc(wb + k * 128, 128), xh_k, k == 0, False)
                        w.mm(self.pb(gb, C), self.bapc(wb + k * 128, 128), xl_k, False, False)
                        w.mm(self.pb(gb, C), self.bapc(wb + 4096 + k * 128, 128), xh_k, False, k == KD - 1,
                             inc=(mmg if k == KD - 1 else None))
                    for k in range(KD):
                        xh_k = self.bapc(XH + k * 512, C)
                        xl_k = self.bapc(XL + k * 512, C)
                        w.mm(self.pb(ub, C), self.bapc(wb + 2048 + k * 128, 128), xh_k, k == 0, False)
                        w.mm(self.pb(ub, C), self.bapc(wb + 2048 + k * 128, 128), xl_k, False, False)
                        w.mm(self.pb(ub, C), self.bapc(wb + 6144 + k * 128, 128), xh_k, False, k == KD - 1,
                             inc=(mmu if k == KD - 1 else None))
                else:
                    for k in range(KD):
                        if hc == 0:
                            w.wait("tensor", xnd, b["xn"] + k + 1)
                        w.mm(self.pb(gb, C), self.apc(self.F_WGU + par * 2 * KD * 128 + k * 128, 128),
                             self.apc(self.F_XN + k * 512, C), k == 0, k == KD - 1,
                             inc=(mmg if k == KD - 1 else None))
                    for k in range(KD):
                        w.mm(self.pb(ub, C), self.apc(self.F_WGU + par * 2 * KD * 128 + KD * 128 + k * 128, 128),
                             self.apc(self.F_XN + k * 512, C), k == 0, k == KD - 1,
                             inc=(mmu if k == KD - 1 else None))
                if mode == "split":
                    w.wait("scalar", sms["w%d" % par], b["w%d" % par] + wcount * (hc // 2 + 1))
                    w.wait("scalar", so, max(0, b["so"] + 32 * hc))
                    w.op("scalar", lambda e, par=par: e.activation(
                        out=self.bapc(SPH, 4096), in_=self.apc(self.F_WGU + par * 2 * KD * 128, 4096),
                        func=AF.Copy), inc=ch)
                    w.wait("vector", ch, b["ch"] + hc + 1)
                    w.op("vector", lambda e, par=par: e.tensor_sub(
                        self.bapc(SPL, 4096), self.apc(self.F_WGU + par * 2 * KD * 128, 4096),
                        self.bapc(SPH, 4096)), inc=cl)
                    w.drain("vector")
                w.wait("scalar", mmg, b["mg"] + hc + 1)
                if hc >= 2:
                    w.wait("scalar", gtd, b["gt"] + hc - 1)
                ssl = self.apc(self.SQT + (hc % 2) * 512, C)
                w.op("scalar", lambda e, ssl=ssl, gb=gb, C=C: e.activation(out=ssl, in_=self.pb(gb, C), func=AF.Silu), inc=sil)
                w.wait("vector", mmu, b["mu"] + hc + 1)
                w.wait("vector", sil, b["si"] + hc + 1)
                if mode == "bf16" and BF16_DOWN:
                    gt = self.apc(self.GTMP, C)
                    w.op("vector", lambda e, gt=gt, ssl=ssl, ub=ub, C=C: e.tensor_tensor(gt, ssl, self.pb(ub, C), ALU.mult))
                    w.drain("vector")
                    w.op("vector", lambda e, j=j, gt=gt, C=C: e.tensor_copy(
                        self.bapc(2 * self.F_GATED + j * 1024, C), gt))
                    w.drain("vector")
                    w.op("vector", lambda e, j=j, gt=gt, C=C: e.tensor_sub(
                        self.bapc(2 * self.F_GATED + j * 1024 + 512, C), gt,
                        self.bapc(2 * self.F_GATED + j * 1024, C)), inc=gtd)
                    w.drain("vector")
                else:
                    gsl = self.apc(self.F_GATED + j * 512, C)
                    w.op("vector", lambda e, gsl=gsl, ssl=ssl, ub=ub, C=C: e.tensor_tensor(gsl, ssl, self.pb(ub, C), ALU.mult), inc=gtd)
                    if j == grp - 1:
                        w.drain("vector")
            if mode == "split" and BF16_DOWN:
                SP2 = 2 * self.RING2 + 4 * KD * 128
                for j in range(grp):
                    hc = gbase[g] + j
                    w.wait("scalar", wdl, b["wd"] + 16 * (g + 1))
                    w.wait("scalar", s2, max(0, b["s2"] + 16 * hc))
                    w.op("scalar", lambda e, j=j: e.activation(
                        out=self.bapc(SP2, D), in_=self.apc(self.F_WD + j * D, D),
                        func=AF.Copy), inc=h2)
                    w.wait("vector", h2, b["h2"] + hc + 1)
                    w.op("vector", lambda e, j=j: e.tensor_sub(
                        self.bapc(SP2 + D, D), self.apc(self.F_WD + j * D, D),
                        self.bapc(SP2, D)), inc=c2)
                    w.drain("vector")
            wd_done = 16 * (gbase[g] + grp) if (mode == "bf16" and BF16_DOWN) else 16 * (g + 1)
            w.wait("tensor", wdl, b["wd"] + wd_done)
            w.wait("tensor", gtd, b["gt"] + gbase[g] + grp)
            for dch in range(KD):
                dpar = dch % 4
                idx = g * KD + dch
                if idx >= 4:
                    w.wait("tensor", acc, b["ac"] + idx - 3)
                if mode == "bf16" and BF16_DOWN:
                    for j in range(grp):
                        wh = self.bapc(2 * self.F_WD + j * 2 * D + dch * 128, 128)
                        wl = self.bapc(2 * self.F_WD + j * 2 * D + D + dch * 128, 128)
                        gh = self.bapc(2 * self.F_GATED + j * 1024, C)
                        gl = self.bapc(2 * self.F_GATED + j * 1024 + 512, C)
                        w.mm(self.pb(4 + dpar, C), wh, gh, j == 0, False)
                        w.mm(self.pb(4 + dpar, C), wh, gl, False, False)
                        w.mm(self.pb(4 + dpar, C), wl, gh, False, j == grp - 1,
                             inc=(dwn if j == grp - 1 else None))
                else:
                    for j in range(grp):
                        w.mm(self.pb(4 + dpar, C), self.apc(self.F_WD + j * D + dch * 128, 128),
                             self.apc(self.F_GATED + j * 512, C), j == 0, j == grp - 1,
                             inc=(dwn if j == grp - 1 else None))
                w.wait("vector", dwn, b["dn"] + idx + 1)
                w.op("vector", lambda e, dch=dch, dpar=dpar, C=C: e.scalar_tensor_tensor(
                    src(dch), self.pb(4 + dpar, C), half, src(dch), ALU.mult, ALU.add), inc=acc)
                if dch == KD - 1:
                    w.drain("vector")

    def emit_ffn_phase(self, w, tiles, dmas, modes=None):
        xld = self.sem("Axl")
        bx = xld.n
        for dst, srcap in dmas:
            w.dma(dst, srcap, inc=xld)
        for e in ("tensor", "scalar", "vector"):
            w.wait(e, xld, bx + 16 * len(dmas))
        for i, (src, C) in enumerate(tiles):
            m = modes[i] if modes else "f32"
            self.emit_ffn_tile(w, src, C, mode=m)

    def emit_qkv(self, w, st):
        ST, KV = self.ST, self.KV
        tag = "B"
        names = ("l0", "l1", "cd", "cr", "pj", "rv", "vr", "td", "tv", "rb", "cl", "rp")
        sms = {s: self.sem(tag + s) for s in names}
        b = self.snap(tag, names)
        csd, csr, pjd, rvd, vrd, trd, tvd, rbd, csl, rpd = (sms[s] for s in names[2:])
        nvb = 0

        if st == 0:
            ranges = [(lambda k, C: self.ring1(k, 0, C), WIN, 0, 0, False, False),
                      (lambda k, C: self.ring1(k, WIN, C), self.ST, WIN, WIN, True, False),
                      (lambda k, C: self.ring2(k, self.ST, C), SINK, self.EXT1 + self.ST, None, False, True)]
        else:
            ranges = [(lambda k, C: self.ring1(k, WIN, C), self.ST, WIN, 0, False, False),
                      (lambda k, C: self.ring2(k, 0, C), self.ST, self.EXT1, WIN, True, False)]

        n_ec = 0
        for ri, (srcf, C, toff, kvc, isq, sink) in enumerate(ranges):
            src = lambda k, srcf=srcf, C=C: srcf(k, C)
            w.wait("vector", pjd)
            bcs, mrow, negm, rrow = self.ln_stats(w, tag + "l", src, C, bcast_m=False)
            w.wait("vector", bcs)
            w.op("vector", lambda e, C=C: e.tensor_copy(self.apc(self.RSB, C), self.pb(5, C)), inc=rbd)
            w.drain("vector")
            w.wait("sync", rpd)
            w.dma(self.apc(self.CS, C), self.cos2[:, toff:toff + C], inc=csl)
            w.dma(self.apc(self.CS + 512, C), self.sin2[:, toff:toff + C], inc=csl)

            projs = [("k", self.wk, NGRP, NGRP * DH, 0, True),
                     ("v", self.wv, NGRP, NGRP * DH, 512, False)]
            if isq:
                projs.append(("q", self.wq, KD, D, 1024, True))
            for pname, wsrc, nec, wcols, sumoff, rope in projs:
                for ec in range(nec):
                    par = n_ec % 2
                    wbase = self.P_WS + par * KD * 128
                    w.wait("sync", pjd, max(0, b["pj"] + n_ec - 1))
                    w.dma(self.ap3(wbase, 128, KD, 1, 128),
                          bass.AP(wsrc.tensor, ec * 128, [[wcols, 128], [128 * wcols, KD], [1, 128]]), inc=sms["l%d" % par])
                    w.wait("tensor", sms["l%d" % par], b["l%d" % par] + 16 * (n_ec // 2 + 1))
                    sumrow = self.apc(self.P_SUM + sumoff + (0 if sumoff == 1024 else ec * 128), 128, parts=1)
                    need_sums = (ri == 0) or pname == "q"
                    if need_sums:
                        w.wait("tensor", csr)
                        for q4 in range(4):
                            w.mm(self.pb(4, 512, parts=1), self.apc(self.ONESC, 1),
                                 self.apc(wbase + q4 * 512, 512), q4 == 0, q4 == 3,
                                 inc=(csd if q4 == 3 else None))
                        w.wait("vector", csd)
                        w.op("vector", lambda e, sumrow=sumrow: e.tensor_reduce(
                            sumrow, bass.AP(self.ps, 4 * 512, [[4096, 1], [1, 128], [128, 4]]),
                            AX.X, ALU.add), inc=csr)
                        w.drain("vector")
                        w.wait("tensor", csr)
                    if n_ec >= 2:
                        w.wait("tensor", rvd, b["rv"] + n_ec - 1)
                    for k in range(KD):
                        w.mm(self.pb(par, C), self.apc(wbase + k * 128, 128), src(k), k == 0, False)
                    w.wait("tensor", rbd)
                    w.mm(self.pb(par, C), sumrow, negm, False, True, inc=pjd)
                    w.wait("vector", pjd, b["pj"] + n_ec + 1)
                    w.wait("vector", csl)
                    bcol = {"k": self.apc(self.BIASC + 2 * KD + ec, 1),
                            "v": self.apc(self.BIASC + 2 * KD + NGRP + ec, 1),
                            "q": self.apc(self.BIASC + ec, 1)}[pname]
                    rt0 = self.apc(self.RT, C)
                    rt1 = self.apc(self.RT + 512, C)
                    rt2 = self.apc(self.RT + 1024, C)
                    w.wait("vector", trd)
                    w.op("vector", lambda e, par=par, C=C: e.tensor_tensor(rt0, self.pb(par, C), self.apc(self.RSB, C), ALU.mult), inc=rvd)
                    w.drain("vector")
                    if pname == "q":
                        w.op("vector", lambda e, bcol=bcol, rt0=rt0: e.tensor_scalar(rt0, rt0, bcol, SCALE, ALU.add, ALU.mult), inc=vrd)
                    else:
                        w.op("vector", lambda e, bcol=bcol, rt0=rt0: e.tensor_scalar_add(rt0, rt0, bcol), inc=vrd)
                    w.drain("vector")
                    if pname == "q":
                        dst = self.apc(self.P_Q + ec * ST, C)
                    elif pname == "k":
                        dst = self.apc(self.KSINK + ec * SINK, SINK) if sink else self.apc(self.P_K + ec * KV + kvc, C)
                    else:
                        dst = None
                    if rope:
                        w.op("vector", lambda e, rt1=rt1, rt0=rt0: e.stream_shuffle(rt1, rt0, SWAP_ADJ))
                        w.drain("vector")
                        w.op("vector", lambda e, rt2=rt2, rt0=rt0, C=C: e.tensor_tensor(rt2, rt0, self.apc(self.CS, C), ALU.mult))
                        w.op("vector", lambda e, rt1=rt1, C=C: e.tensor_tensor(rt1, rt1, self.apc(self.CS + 512, C), ALU.mult))
                        w.drain("vector")
                        w.op("vector", lambda e, dst=dst, rt2=rt2, rt1=rt1: e.tensor_add(dst, rt2, rt1), inc=rpd)
                        w.drain("vector")
                    else:
                        nblk = (C + 127) // 128
                        for jj in range(nblk):
                            cc = min(128, C - jj * 128)
                            tb = 2 + nvb % 2
                            w.wait("tensor", vrd)
                            if nvb >= 2:
                                w.wait("tensor", tvd, b["tv"] + nvb - 1)
                            w.tr(self.pb(tb, 128, parts=cc),
                                 self.apc(self.RT + jj * 128, cc), self.apc(self.IDENT, 128), inc=trd)
                            if sink:
                                vdst = self.apc(self.VSINK + ec * 128, 128, parts=cc)
                            else:
                                vdst = self.apc(self.P_V + (kvc // 128 + jj) * 512 + ec * 128, 128, parts=cc)
                            w.wait("scalar", trd)
                            w.op("scalar", lambda e, vdst=vdst, tb=tb, cc=cc: e.activation(
                                out=vdst, in_=self.pb(tb, 128, parts=cc), func=AF.Copy), inc=tvd)
                            nvb += 1
                    n_ec += 1

    def emit_attn(self, w, st):
        ST, KV, NQT, MCOL = self.ST, self.KV, self.NQT, self.MCOL
        tag = "C"
        names = ("ml", "qk", "mk", "ex", "sm", "pn", "pt", "pe", "pv", "av", "o0", "o1", "oo", "xb")
        sms = {s: self.sem(tag + s) for s in names}
        b = self.snap(tag, names)
        mld, qkd, mkd, exd, smd, pnd, ptd, pte, pvd, ave = (sms[s] for s in names[:10])
        ood, xbd = sms["oo"], sms["xb"]

        nmx = self.apc(self.NMX, 1)
        ssm = self.apc(self.NMX + 1, 1)
        rcp = self.apc(self.NMX + 2, 1)
        NIT = NQT * NHEAD

        for s in ("Brp", "Btv", "Brv", "Bcr", "Brb"):
            sm = self.sem(s)
            w.wait("tensor", sm, sm.n)
        w.wait("sync", self.sem("Brp"), self.sem("Brp").n)

        def emit_qk(it):
            i, h = divmod(it, NHEAD)
            g = h // GSZ
            p = it % 2
            if i > 0 and h == 0:
                w.wait("sync", mkd, b["mk"] + i * NHEAD)
                w.dma(self.apc(self.P_MASK, MCOL),
                      bass.AP(self.maskd.tensor, (st * NQT + i) * 128 * MCOL, [[MCOL, 128], [1, MCOL]]),
                      inc=mld)
            if it >= 2:
                w.wait("tensor", exd, b["ex"] + it - 1)
            qsl = self.apc(self.P_Q + h * ST + i * 128, 128)
            w.mm(self.pb(2 * p, 512), qsl, self.apc(self.P_K + g * KV + i * 128, 512), True, True)
            w.mm(self.pb(2 * p + 1, 128), qsl, self.apc(self.P_K + g * KV + i * 128 + 512, 128), True, False)
            w.mm(self.pb(2 * p + 1, SINK, c0=128), qsl, self.apc(self.KSINK + g * SINK, SINK), False, True, inc=qkd)

        def emit_softmax(it):
            p = it % 2
            w.wait("vector", qkd, b["qk"] + it + 1)
            w.wait("vector", mld)
            if it >= 1:
                w.wait("vector", pnd, b["pn"] + it)
            w.op("vector", lambda e, p=p: e.tensor_tensor(self.psp(p * 1024, 644), self.psp(p * 1024, 644),
                                                          self.apc(self.P_MASK, 644), ALU.add))
            w.drain("vector")
            w.op("vector", lambda e, p=p: e.tensor_reduce(nmx, self.psp(p * 1024, 644), AX.X, ALU.max, negate=True), inc=mkd)
            w.wait("scalar", mkd, b["mk"] + it + 1)
            w.op("scalar", lambda e, p=p: e.activation(out=self.apc(self.P_P, 644), in_=self.psp(p * 1024, 644),
                                                       func=AF.Exp, bias=nmx, scale=1.0, accum_out=ssm), inc=exd)
            w.wait("vector", exd, b["ex"] + it + 1)
            w.op("vector", lambda e: e.reciprocal(rcp, ssm), inc=smd)
            w.wait("scalar", smd, b["sm"] + it + 1)
            w.wait("scalar", ptd, b["pt"] + it)
            w.op("scalar", lambda e: e.activation(out=self.apc(self.P_PN, 644), in_=self.apc(self.P_P, 644),
                                                  func=AF.Copy, scale=rcp), inc=pnd)

        def emit_pv(it):
            i, h = divmod(it, NHEAD)
            g = h // GSZ
            w.wait("tensor", pnd, b["pn"] + it + 1)
            if it >= 1:
                w.wait("tensor", pte, b["pe"] + it)
            for j in range(4):
                w.tr(self.pb(4, 128, c0=j * 128), self.apc(self.P_PN + j * 128, 128), self.apc(self.IDENT, 128))
            w.tr(self.pb(5, 128), self.apc(self.P_PN + 512, 128), self.apc(self.IDENT, 128))
            w.tr(self.pb(5, 128, c0=128, parts=SINK), self.apc(self.P_PN + 640, SINK), self.apc(self.IDENT, 128), inc=ptd)
            w.wait("scalar", ptd, b["pt"] + it + 1)
            w.op("scalar", lambda e: e.activation(out=self.apc(self.P_PT, 512), in_=self.pb(4, 512), func=AF.Copy))
            w.op("scalar", lambda e: e.activation(out=self.apc(self.P_PT + 512, 256), in_=self.pb(5, 256), func=AF.Copy), inc=pte)
            w.drain("scalar")
            par = 6 + it % 2
            w.wait("tensor", pte, b["pe"] + it + 1)
            if it >= 2:
                w.wait("tensor", ave, b["av"] + it - 1)
            for j in range(5):
                w.mm(self.pb(par, 128), self.apc(self.P_V + (i + j) * 512 + g * 128, 128),
                     self.apc(self.P_PT + j * 128, 128), j == 0, False)
            w.mm(self.pb(par, 128), self.apc(self.VSINK + g * 128, 128, parts=SINK),
                 self.apc(self.P_PT + 640, 128, parts=SINK), False, True, inc=pvd)
            w.wait("scalar", pvd, b["pv"] + it + 1)
            w.op("scalar", lambda e, h=h, i=i, par=par: e.activation(
                out=self.apc(self.P_Q + h * ST + i * 128, 128), in_=self.pb(par, 128), func=AF.Copy), inc=ave)

        w.dma(self.apc(self.P_MASK, MCOL),
              bass.AP(self.maskd.tensor, (st * NQT) * 128 * MCOL, [[MCOL, 128], [1, MCOL]]),
              inc=mld)
        for it in range(NIT + 1):
            if it < NIT:
                emit_qk(it)
                emit_softmax(it)
            if it >= 1:
                emit_pv(it - 1)

        res = (lambda k, c0, c: self.ring1(k, WIN + c0, c)) if st == 0 else (lambda k, c0, c: self.ring2(k, c0, c))
        xb = (lambda k, c0, c: self.ring1(k, c0, c)) if st == 0 else (lambda k, c0, c: self.ring1(k, WIN + c0, c))
        for ec in range(KD):
            par = 6 + ec % 2
            wbase = self.P_WS + (ec % 2) * KD * 128
            w.wait("sync", ood, max(0, b["oo"] + ec - 1))
            w.dma(self.ap3(wbase, 128, KD, 1, 128),
                  bass.AP(self.wo.tensor, ec * 128, [[D, 128], [128 * D, KD], [1, 128]]), inc=sms["o%d" % (ec % 2)])
            w.wait("tensor", sms["o%d" % (ec % 2)], b["o%d" % (ec % 2)] + 16 * (ec // 2 + 1))
            if ec == 0:
                w.wait("tensor", ave, b["av"] + NQT * NHEAD)
            if ec >= 2:
                w.wait("tensor", xbd, b["xb"] + ec - 1)
            for cc in range(KD):
                w.mm(self.pb(par, ST), self.apc(wbase + cc * 128, 128),
                     self.apc(self.P_Q + cc * ST, ST),
                     cc == 0, cc == KD - 1, inc=(ood if cc == KD - 1 else None))
            w.wait("vector", ood, b["oo"] + ec + 1)
            for ct in range(NQT):
                w.op("vector", lambda e, ec=ec, ct=ct, par=par: e.tensor_add(
                    xb(ec, ct * 128, 128), res(ec, ct * 128, 128), self.pb(par, 128, c0=ct * 128)))
            w.drain("vector")
            w.op("vector", lambda e, ec=ec, xb=xb, ST=ST: e.tensor_scalar_add(
                xb(ec, 0, ST), xb(ec, 0, ST), self.apc(self.BIASC + KD + ec, 1)), inc=xbd)
            w.drain("vector")

    def emit_out(self, w):
        so = self.sem("out")
        bo = so.n
        w.dma(bass.AP(self.yt.tensor, 0, [[self.OWN, 128], [128 * self.OWN, KD], [1, self.ST]]),
              self.ap3(self.RING1, self.EXT1, KD, 1, self.ST), inc=so)
        w.dma(bass.AP(self.yt.tensor, self.ST, [[self.OWN, 128], [128 * self.OWN, KD], [1, self.ST]]),
              self.ap3(self.RING1 + WIN, self.EXT1, KD, 1, self.ST), inc=so)
        w.wait("sync", so, bo + 32)

    def build(self):
        ST = self.ST
        self.phase("pre", self.emit_preload)
        a1_dma = [(self.ap3(self.RING1, self.EXT1, KD, 1, self.EXT1),
                   bass.AP(self.xe.tensor, 0, [[self.EXT1, 128], [128 * self.EXT1, KD], [1, self.EXT1]]))]
        a1_tiles = [((lambda k, c0=c0, C=C: self.ring1(k, c0, C)), C) for c0, C in balanced_split(self.EXT1)]
        a1_modes = (["split"] + ["bf16"] * (len(a1_tiles) - 1)) if BF16_GU else None
        self.phase("A1", lambda w: self.emit_ffn_phase(w, a1_tiles, a1_dma, a1_modes))
        a2_dma = [(self.ap3(self.RING2, self.R2, KD, 1, ST),
                   bass.AP(self.xo2.tensor, 0, [[ST, 128], [128 * ST, KD], [1, ST]])),
                  (self.ap3(self.RING2 + ST, self.R2, KD, 1, SINK),
                   bass.AP(self.xsink.tensor, 0, [[SINK, 128], [128 * SINK, KD], [1, SINK]]))]
        a2_tiles = [((lambda k, c0=c0, C=C: self.ring2(k, c0, C)), C) for c0, C in balanced_split(self.R2)]
        bfm = lambda ts: (["bf16"] * len(ts)) if BF16_GU else None
        self.phase("A2", lambda w: self.emit_ffn_phase(w, a2_tiles, a2_dma, bfm(a2_tiles)))
        self.phase("BC1", lambda w: (self.emit_qkv(w, 0), self.emit_attn(w, 0)))
        d1_tiles = [((lambda k, c0=c0, C=C: self.ring1(k, c0, C)), C) for c0, C in balanced_split(ST)]
        self.phase("D1", lambda w: self.emit_ffn_phase(w, d1_tiles, [], bfm(d1_tiles)))
        self.phase("BC2", lambda w: (self.emit_qkv(w, 1), self.emit_attn(w, 1)))
        d2_tiles = [((lambda k, c0=c0, C=C: self.ring1(k, WIN + c0, C)), C) for c0, C in balanced_split(ST)]
        self.phase("D2", lambda w: self.emit_ffn_phase(w, d2_tiles, [], bfm(d2_tiles)))
        self.phase("out", self.emit_out)
        return self.nc



def rope_tables(positions):
    half = 64
    inv = 1.0 / (10000.0 ** (np.arange(half, dtype=np.float32) * 2.0 / DH))
    ang = positions[None, :].astype(np.float32) * inv[:, None]
    c, s = np.cos(ang), np.sin(ang)
    cos2 = np.empty((128, ang.shape[1]), np.float32)
    sin2 = np.empty((128, ang.shape[1]), np.float32)
    cos2[0::2] = c
    cos2[1::2] = c
    sin2[0::2] = -s
    sin2[1::2] = s
    return cos2, sin2


def perm_cols():
    p = np.empty(DH, np.int64)
    p[0::2] = np.arange(64)
    p[1::2] = np.arange(64) + 64
    return p


def build_masks(own, p0):
    STl = own // 2
    NQT = STl // 128
    MCOL = WIN + 128 + SINK
    out = np.zeros((2 * NQT, 128, MCOL), np.float32)
    for st in range(2):
        for i in range(NQT):
            q0g = p0 + st * STl + i * 128
            rows = q0g + np.arange(128)[:, None]
            kw = (q0g - WIN) + np.arange(WIN + 128)[None, :]
            valid = (kw >= 0) & (kw <= rows) & (kw >= rows - WIN)
            m = out[st * NQT + i]
            m[:, :WIN + 128] = np.where(valid, 0.0, NEG)
            ks = np.arange(SINK)[None, :]
            vs = (ks <= rows) & (rows - ks > WIN)
            m[:, WIN + 128:] = np.where(vs, 0.0, NEG)
    return out


_BUILD_CACHE = {}


def get_nc(own):
    if own not in _BUILD_CACHE:
        _BUILD_CACHE[own] = Builder(own).build()
    return _BUILD_CACHE[own]


def make_core_inputs(xt, weights, own, b, p0, ttot):
    T = xt.shape[2]
    ext0 = p0 - WIN
    xe = np.zeros((D, WIN + own // 2), np.float32)
    lo = max(0, ext0)
    xe[:, lo - ext0:] = xt[b, :, lo:p0 + own // 2]
    d = dict(weights)
    d["xe"] = xe
    d["xo2"] = np.ascontiguousarray(xt[b, :, p0 + own // 2:p0 + own])
    d["xsink"] = np.ascontiguousarray(xt[b, :, 0:SINK])
    pos = np.concatenate([np.maximum(np.arange(ext0, p0 + own // 2), 0),
                          np.arange(p0 + own // 2, p0 + own),
                          np.arange(SINK)]).astype(np.float32)
    assert pos.shape[0] == ttot
    c2, s2 = rope_tables(pos)
    d["cos2"] = c2
    d["sin2"] = s2
    d["maskd"] = build_masks(own, p0).reshape(-1, WIN + 128 + SINK)
    return d


def prep_weights(L0_attn_Wq, L0_attn_bq, L0_attn_Wk, L0_attn_bk, L0_attn_Wv,
                 L0_attn_bv, L0_attn_Wo, L0_attn_bo, L0_ffn_w_gate,
                 L0_ffn_w_up, L0_ffn_w_down):
    pc = perm_cols()
    qperm = np.concatenate([h * DH + pc for h in range(NHEAD)])
    kperm = np.concatenate([g * DH + pc for g in range(NGRP)])
    w = {}
    w["wgt"] = np.ascontiguousarray(np.asarray(L0_ffn_w_gate).T)
    w["wut"] = np.ascontiguousarray(np.asarray(L0_ffn_w_up).T)
    w["wdt"] = np.ascontiguousarray(np.asarray(L0_ffn_w_down).T)
    w["wq"] = np.ascontiguousarray(np.asarray(L0_attn_Wq)[:, qperm])
    w["wk"] = np.ascontiguousarray(np.asarray(L0_attn_Wk)[:, kperm])
    w["wv"] = np.ascontiguousarray(np.asarray(L0_attn_Wv))
    w["wo"] = np.ascontiguousarray(np.asarray(L0_attn_Wo))
    w["bqc"] = np.ascontiguousarray(np.asarray(L0_attn_bq)[qperm].reshape(KD, 128).T)
    w["bkc"] = np.ascontiguousarray(np.asarray(L0_attn_bk)[kperm].reshape(NGRP, 128).T)
    w["bvc"] = np.ascontiguousarray(np.asarray(L0_attn_bv).reshape(NGRP, 128).T)
    w["boc"] = np.ascontiguousarray(np.asarray(L0_attn_bo).reshape(KD, 128).T)
    return w


def kernel(L0_attn_x, L0_attn_Wq, L0_attn_bq, L0_attn_Wk, L0_attn_bk,
           L0_attn_Wv, L0_attn_bv, L0_attn_Wo, L0_attn_bo, L0_ffn_w_gate,
           L0_ffn_w_up, L0_ffn_w_down):
    x = np.asarray(L0_attn_x, dtype=np.float32)
    B, T, _ = x.shape
    n_cores = 8
    chunks = n_cores // B
    own = T // chunks
    xt = np.ascontiguousarray(x.transpose(0, 2, 1))
    weights = prep_weights(L0_attn_Wq, L0_attn_bq, L0_attn_Wk, L0_attn_bk,
                           L0_attn_Wv, L0_attn_bv, L0_attn_Wo, L0_attn_bo,
                           L0_ffn_w_gate, L0_ffn_w_up, L0_ffn_w_down)
    ttot = WIN + own + SINK
    in_maps = []
    for c in range(n_cores):
        b, j = c // chunks, c % chunks
        in_maps.append(make_core_inputs(xt, weights, own, b, j * own, ttot))
    nc = get_nc(own)
    res = run_bass_kernel_spmd(nc, in_maps, core_ids=list(range(n_cores)))
    yt = np.empty((B, D, T), np.float32)
    for c in range(n_cores):
        b, j = c // chunks, c % chunks
        yt[b, :, j * own:(j + 1) * own] = res.results[c]["yt"]
    return np.ascontiguousarray(yt.transpose(0, 2, 1))
